# version 12
# baseline (speedup 1.0000x reference)
import numpy as np
import ml_dtypes
from contextlib import ExitStack


class _PairBuf:
    def __init__(self, lo, hi, half):
        self.lo, self.hi, self.half = lo, hi, half

    def __getitem__(self, idx):
        p, c = idx
        if c.start >= self.half:
            return self.hi[p, c.start - self.half:c.stop - self.half]
        assert c.stop <= self.half
        return self.lo[p, c.start:c.stop]

import concourse.bass as bass
import concourse.mybir as mybir
from concourse.bass_utils import run_bass_kernel_spmd

F32 = mybir.dt.float32
BF16 = mybir.dt.bfloat16
AF = mybir.ActivationFunctionType
ALU = mybir.AluOpType
AXX = mybir.AxisListType.X
BFNP = ml_dtypes.bfloat16
AP = bass.AP

B, N, C = 4, 4096, 1024
E, Hf = 8, 4096
TOK, TILE = 2048, 512
NT_A, NT = 8, 4
CCH = 8
HCH = 16
EPS_ATTN, EPS_LN = 1e-15, 1e-5

CB_ID, CB_ONES, CB_SELH = 0, 128, 256
CB_SELG = CB_SELH + 8 * 128
CBW = CB_SELG + 8 * 128


def host_consts():
    cbf = np.zeros((128, CBW), dtype=np.float32)
    cbf[:, CB_ID:CB_ID + 128] = np.eye(128)
    cbf[:, CB_ONES:CB_ONES + 128] = 1.0
    for c in range(CCH):
        blk = np.zeros((128, 128), dtype=np.float32)
        for hl in range(4):
            blk[4 * c + hl, 32 * hl:32 * hl + 32] = 1.0
        cbf[:, CB_SELH + 128 * c:CB_SELH + 128 * (c + 1)] = blk
    for e in range(E):
        blk = np.zeros((128, 128), dtype=np.float32)
        blk[e, :] = 1.0
        cbf[:, CB_SELG + 128 * e:CB_SELG + 128 * (e + 1)] = blk
    return cbf.astype(BFNP), np.eye(128, dtype=np.float32)


class Ctr:
    def __init__(self, ctx, nc, name):
        self.sem = ctx.enter_context(nc.semaphore(name))
        self.n = 0

    def inc(self, instr, k=1):
        instr.then_inc(self.sem, k)
        self.n += k
        return self.n

    def dma(self, eng, out, in_):
        eng.wait_ge(self.sem, self.n)
        eng.dma_start(out, in_).then_inc(self.sem, 16)
        self.n += 16
        return self.n


def build_program(stop_after=None, debug=()):
    nc = bass.Bass("TRN2", target_bir_lowering=False)
    dbg_specs = []

    xT = nc.dram_tensor("xT", [C, N], F32, kind="ExternalInput")
    wqkvT = nc.dram_tensor("wqkvT", [C, 3 * C], F32, kind="ExternalInput")
    wprojT = nc.dram_tensor("wprojT", [C, C], F32, kind="ExternalInput")
    bprojH = nc.dram_tensor("bprojH", [128, 8], F32, kind="ExternalInput")
    rwH = nc.dram_tensor("rwH", [128, 8 * E], F32, kind="ExternalInput")
    upw = nc.dram_tensor("upw", [E, C, Hf], F32, kind="ExternalInput")
    downw = nc.dram_tensor("downw", [E, Hf, C], F32, kind="ExternalInput")
    cbfH = nc.dram_tensor("cbfH", [128, CBW], BF16, kind="ExternalInput")
    cf32H = nc.dram_tensor("cf32H", [128, 128], F32, kind="ExternalInput")
    yscr = nc.dram_tensor("yscr", [C, TOK], BF16)
    outT = nc.dram_tensor("outT", [C, TOK], F32, kind="ExternalOutput")

    ctx = ExitStack()
    with ctx:
        Ecm = ctx.enter_context

        sg = Ctr(ctx, nc, "sg")
        sy = Ctr(ctx, nc, "sy")
        st = Ctr(ctx, nc, "st")
        sv = Ctr(ctx, nc, "sv")
        sc = Ctr(ctx, nc, "sc")

        ps = [Ecm(nc.psum_tensor(f"ps{i}", [128, 512], F32)) for i in range(8)]

        cbf = Ecm(nc.sbuf_tensor("cbf", [128, CBW], BF16))
        cf32 = Ecm(nc.sbuf_tensor("cf32", [128, 128], F32))
        bproj = Ecm(nc.sbuf_tensor("bproj", [128, 8], F32))
        epsb = Ecm(nc.sbuf_tensor("epsb", [128, 1], F32))
        rw = Ecm(nc.sbuf_tensor("rw", [128, 8 * E], BF16))
        vkdiag = Ecm(nc.sbuf_tensor("vkdiag", [128, 8 * 128], BF16))
        ks_st = Ecm(nc.sbuf_tensor("ks_st", [128, 8 * 128], BF16))
        gates_fm = Ecm(nc.sbuf_tensor("gates_fm", [8, TOK], BF16))
        u_sb = Ecm(nc.sbuf_tensor("u_sb", [1, 1024], F32))
        msq_sb = Ecm(nc.sbuf_tensor("msq_sb", [1, 512], F32))
        var_sb = Ecm(nc.sbuf_tensor("var_sb", [1, 512], F32))
        rst_sb = Ecm(nc.sbuf_tensor("rst_sb", [1, 512], F32))
        a_bf = Ecm(nc.sbuf_tensor("a_bf", [1, 512], BF16))
        b_bf = Ecm(nc.sbuf_tensor("b_bf", [1, 512], BF16))

        ident_bf = cbf[:, CB_ID:CB_ID + 128]
        ones_bf = cbf[:, CB_ONES:CB_ONES + 128]

        blk = Ecm(nc.Block())

        def tbar(pos):
            tb = getattr(build_program, "TBAR", ("norm:v",))
            if pos in tb or (pos + ":v") in tb:
                nc.tensor.wait_ge(sv.sem, sv.n)
            if pos in tb or (pos + ":s") in tb:
                nc.tensor.wait_ge(sc.sem, sc.n)

        def vguard():
            nc.vector.engine_nop().then_inc(sv.sem, 1)
            sv.n += 1
            nc.vector.wait_ge(sv.sem, sv.n)

        def dump(tag, src_ap, shape, dtype=F32):
            if tag not in debug:
                return
            d = nc.dram_tensor("dbg_" + tag, list(shape), dtype,
                               kind="ExternalOutput")
            for ctr in (st, sv, sc, sg):
                nc.sync.wait_ge(ctr.sem, ctr.n)
            sy.dma(nc.sync, d[:, :], src_ap)
            for eng in (nc.tensor, nc.vector, nc.scalar, nc.gpsimd):
                eng.wait_ge(sy.sem, sy.n)
            dbg_specs.append(tag)

        bankdep = {}

        def bank_w(*banks):
            for b in banks:
                for (sem_, ev_) in bankdep.pop(id(b), []):
                    nc.tensor.wait_ge(sem_, ev_)

        def bank_r(b, ctr, ev):
            bankdep.setdefault(id(b), []).append((ctr.sem, ev))

        ln_state = {"ln_ev": 0}
        def ln_stats(x_bf_t, xsq_t, ev_ready_val, bank_x, bank_sq, half=0,
                     ev_sc_free=None):
            h5 = half * 512
            nc.tensor.wait_ge(sv.sem, ev_ready_val)
            if ev_sc_free:
                nc.tensor.wait_ge(sc.sem, ev_sc_free)
            bank_w(bank_x, bank_sq)
            for cc in range(CCH):
                nc.tensor.matmul(bank_x[0:1, 0:512], ones_bf[:, 0:1],
                                 x_bf_t[:, cc * 512:(cc + 1) * 512],
                                 start=(cc == 0), stop=(cc == CCH - 1))
            for cc in range(CCH):
                li = nc.tensor.matmul(bank_sq[0:1, 0:512], ones_bf[:, 0:1],
                                      xsq_t[:, cc * 512:(cc + 1) * 512],
                                      start=(cc == 0), stop=(cc == CCH - 1))
            ev = st.inc(li)
            nc.vector.wait_ge(st.sem, ev)
            nc.vector.tensor_scalar(u_sb[0:1, 2 * h5:2 * h5 + 512],
                                    bank_x[0:1, 0:512],
                                    1.0 / 1024.0, None, ALU.mult)
            li = nc.vector.tensor_scalar(u_sb[0:1, 2 * h5 + 512:2 * h5 + 1024],
                                         bank_sq[0:1, 0:512],
                                         1.0 / 1024.0, None, ALU.mult)
            ev_u = sv.inc(li)
            bank_r(bank_x, sv, ev_u)
            bank_r(bank_sq, sv, ev_u)
            nc.vector.wait_ge(sv.sem, ev_u)
            nc.vector.tensor_tensor(msq_sb[0:1, h5:h5 + 512],
                                    u_sb[0:1, 2 * h5:2 * h5 + 512],
                                    u_sb[0:1, 2 * h5:2 * h5 + 512], ALU.mult)
            vguard()
            nc.vector.wait_ge(sc.sem, ln_state["ln_ev"])
            li = nc.vector.scalar_tensor_tensor(
                var_sb[0:1, h5:h5 + 512],
                u_sb[0:1, 2 * h5 + 512:2 * h5 + 1024],
                EPS_LN, msq_sb[0:1, h5:h5 + 512],
                ALU.add, ALU.subtract)
            ev_v = sv.inc(li)
            nc.scalar.wait_ge(sv.sem, ev_v)
            li = nc.scalar.activation(rst_sb[0:1, h5:h5 + 512],
                                      var_sb[0:1, h5:h5 + 512], AF.Ln)
            ln_state["ln_ev"] = sc.inc(li)
            nc.scalar.wait_ge(sc.sem, ln_state["ln_ev"])
            li = nc.scalar.activation(rst_sb[0:1, h5:h5 + 512],
                                      rst_sb[0:1, h5:h5 + 512], AF.Exp,
                                      scale=-0.5)
            ev_s = sc.inc(li)
            nc.vector.wait_ge(sc.sem, ev_s)
            li = nc.vector.tensor_copy(a_bf[0:1, h5:h5 + 512],
                                       rst_sb[0:1, h5:h5 + 512])
            sv.inc(li)
            li = nc.vector.tensor_tensor(b_bf[0:1, h5:h5 + 512],
                                         u_sb[0:1, 2 * h5:2 * h5 + 512],
                                         rst_sb[0:1, h5:h5 + 512], ALU.mult)
            return sv.inc(li)

        def ln_bcast(ev_ab, bankA, bankB, abc_bf, bbc_bf, half=0,
                     ev_banks_free_sc=None):
            h5 = half * 512
            nc.tensor.wait_ge(sv.sem, ev_ab)
            if ev_banks_free_sc:
                nc.tensor.wait_ge(sc.sem, ev_banks_free_sc)
            bank_w(bankA, bankB)
            nc.tensor.matmul(bankA[:, 0:512], ones_bf[0:1, 0:128],
                             a_bf[0:1, h5:h5 + 512], start=True, stop=True)
            li = nc.tensor.matmul(bankB[:, 0:512], ones_bf[0:1, 0:128],
                                  b_bf[0:1, h5:h5 + 512], start=True,
                                  stop=True)
            ev = st.inc(li)
            if abc_bf is None:
                return ("st", ev)
            nc.scalar.wait_ge(st.sem, ev)
            sc.inc(nc.scalar.copy(abc_bf[:, :], bankA[:, 0:512]))
            li = nc.scalar.copy(bbc_bf[:, :], bankB[:, 0:512])
            ev2 = sc.inc(li)
            bank_r(bankA, sc, ev2)
            bank_r(bankB, sc, ev2)
            return ("sc", ev2)

        def ln_apply(x_bf_t, abc_bf, bbc_bf, dst, ev_abc):
            eng_tag, ev = ev_abc
            nc.vector.wait_ge(sc.sem if eng_tag == "sc" else st.sem, ev)
            for cc in range(CCH):
                nc.vector.tensor_tensor(dst[:, cc * 512:(cc + 1) * 512],
                                        x_bf_t[:, cc * 512:(cc + 1) * 512],
                                        abc_bf[:, :], ALU.mult)
            vguard()
            for cc in range(CCH):
                li = nc.vector.tensor_tensor(dst[:, cc * 512:(cc + 1) * 512],
                                             dst[:, cc * 512:(cc + 1) * 512],
                                             bbc_bf[:, :], ALU.subtract)
                ev = sv.inc(li)
            return ev

        def phase_barrier():
            marks = [(st, st.n), (sv, sv.n), (sc, sc.n), (sg, sg.n), (sy, sy.n)]
            for eng in (nc.tensor, nc.vector, nc.scalar, nc.gpsimd, nc.sync):
                for ctr, val in marks:
                    eng.wait_ge(ctr.sem, val)

        sy.dma(nc.sync, cbf[:, :], cbfH[:, :])
        sy.dma(nc.sync, cf32[:, :], cf32H[:, :])
        sy.dma(nc.sync, bproj[:, :], bprojH[:, :])
        ev_consts_sy = sy.n
        sg.dma(nc.gpsimd, rw[:, :], rwH[:, :])
        ev_rw = sg.n
        nc.vector.memset(epsb[:, :], EPS_ATTN)
        ev_epsb = sv.inc(nc.vector.engine_nop())

        w1 = Ecm(nc.sbuf_tensor("w1", [128, CCH * 2048], BF16))
        with ExitStack() as actx:
            Ea = actx.enter_context
            wqkv = Ea(nc.sbuf_tensor("wqkv", [128, CCH * 3072], BF16))
            x_bf2 = [Ea(nc.sbuf_tensor(f"xbfA{i}", [128, 4096], BF16)) for i in range(2)]
            xsq2 = [Ea(nc.sbuf_tensor(f"xsqA{i}", [128, 4096], BF16)) for i in range(2)]
            xln2 = [Ea(nc.sbuf_tensor(f"xlnA{i}", [128, 4096], BF16)) for i in range(2)]
            k_bf2 = [Ea(nc.sbuf_tensor(f"kbf{i}", [128, 4096], BF16)) for i in range(2)]
            v33_2 = [Ea(nc.sbuf_tensor(f"v33_{i}", [128, 4224], BF16)) for i in range(2)]
            a_bcA = Ea(nc.sbuf_tensor("a_bcA", [128, 512], BF16))
            b_bcA = Ea(nc.sbuf_tensor("b_bcA", [128, 512], BF16))
            vk_sb = Ea(nc.sbuf_tensor("vk_sb", [128, 8 * 132], F32))

            ev_xbf, ev_xln_t, ev_stats_t, ev_abc_t = {}, {}, {}, {}
            ev_kvevac, ev_vk_sub = {}, {}

            def load_x(t):
                if t >= 2:
                    nc.gpsimd.wait_ge(sv.sem, ev_xln_t[t - 2])
                sg.dma(nc.gpsimd,
                    AP(x_bf2[t % 2], 0, [[4096, 128], [512, CCH], [1, 512]]),
                    AP(xT, t * 512, [[N, 128], [128 * N, CCH], [1, 512]]))
                ev_xbf[t] = sg.n

            def ln_chain_a(t):
                bi = t % 2
                nc.vector.wait_ge(sg.sem, ev_xbf[t])
                if t >= 2:
                    nc.vector.wait_ge(st.sem, ev_stats_t[t - 1])
                li = nc.vector.tensor_tensor(xsq2[bi][:, :], x_bf2[bi][:, :],
                                             x_bf2[bi][:, :], ALU.mult)
                ev_sq = sv.inc(li)
                if t == 0:
                    nc.tensor.wait_ge(sy.sem, ev_consts_sy)
                ev_ab = ln_stats(x_bf2[bi], xsq2[bi], ev_sq, ps[3], ps[4],
                                 ev_sc_free=ev_abc_t.get(t - 1))
                ev_stats_t[t] = st.n
                return ev_ab

            def ln_finish_a(t, ev_ab):
                bi = t % 2
                ev_abc = ln_bcast(ev_ab, ps[3], ps[4], a_bcA, b_bcA)
                ev_abc_t[t] = ev_abc[1]
                if t >= 2:
                    nc.vector.wait_ge(st.sem, ev_kv_last_mm[t - 2])
                ev_xln_t[t] = ln_apply(x_bf2[bi], a_bcA, b_bcA, xln2[bi], ev_abc)

            def kv_sub(t, sub):
                bi = t % 2
                for og in range(4):
                    gi = t * 16 + sub * 4 + og
                    if t == 0 and sub == 0:
                        nc.tensor.wait_ge(sg.sem, wqkv_parts[og])
                    bk = ps[5 + (gi % 3)]
                    if gi - 3 in ev_kvevac:
                        nc.tensor.wait_ge(sc.sem, ev_kvevac[gi - 3])
                    for cc in range(CCH):
                        li = nc.tensor.matmul(
                            bk[:, 0:512],
                            xln2[bi][:, cc * 512 + sub * 128:
                                     cc * 512 + sub * 128 + 128],
                            wqkv[:, cc * 3072 + 1024 + 512 * og:
                                 cc * 3072 + 1024 + 512 * (og + 1)],
                            start=(cc == 0), stop=(cc == CCH - 1))
                    ev_mm = st.inc(li)
                    ev_kv_last_mm[t] = ev_mm
                    nc.scalar.wait_ge(st.sem, ev_mm)
                    if sub == 0 and og == 0:
                        if t <= 1:
                            nc.scalar.wait_ge(sv.sem, ev_v33init)
                        if (t - 2, 3) in ev_vk_sub:
                            nc.scalar.wait_ge(st.sem, ev_vk_sub[(t - 2, 3)])
                    if og < 2:
                        li = nc.scalar.activation(
                            k_bf2[bi][:, sub * 1024 + og * 512:
                                      sub * 1024 + (og + 1) * 512],
                            bk[:, 0:512], AF.Relu)
                    else:
                        li = nc.scalar.copy(
                            AP(v33_2[bi], sub * 1056 + (og - 2) * 528,
                               [[4224, 128], [33, 16], [1, 32]]),
                            bk[:, 0:512])
                    ev_kvevac[gi] = sc.inc(li)

                nc.tensor.wait_ge(sc.sem, ev_kvevac[t * 16 + sub * 4 + 3])
                for gg in range(8):
                    li = nc.tensor.matmul(
                        ps[gg // 3][:, (gg % 3) * 132:(gg % 3) * 132 + 132],
                        k_bf2[bi][:, sub * 1024 + gg * 128:
                                  sub * 1024 + (gg + 1) * 128],
                        AP(v33_2[bi], sub * 1056 + gg * 132,
                           [[4224, 128], [1, 132]]),
                        start=(t == 0 and sub == 0 and gg % 3 == 0),
                        stop=(t == NT_A - 1 and sub == 3
                              and (gg % 3 == 2 or gg == 7)))
                ev_vk_sub[(t, sub)] = st.inc(li)

            ev_kv_last_mm = {}
            wqkv_parts = [None] * 4

            def load_wqkv_part(i):
                sg.dma(nc.gpsimd,
                    AP(wqkv, 1024 + i * 512,
                       [[CCH * 3072, 128], [3072, CCH], [1, 512]]),
                    AP(wqkvT, 1024 + i * 512,
                       [[3072, 128], [128 * 3072, CCH], [1, 512]]))
                wqkv_parts[i] = sg.n

            load_x(0)
            load_wqkv_part(0)
            load_x(1)
            for i in range(1, 4):
                load_wqkv_part(i)
            ev_wqkv = sg.n
            for i in range(2):
                li = nc.vector.memset(
                    AP(v33_2[i], 32, [[4224, 128], [33, 128], [1, 1]]), 1.0)
                ev_v33init = sv.inc(li)

            ev_ab0 = ln_chain_a(0)
            ln_finish_a(0, ev_ab0)

            for t in range(NT_A):
                nc.tensor.wait_ge(sv.sem, ev_xln_t[t])
                if t + 2 < NT_A:
                    load_x(t + 2)
                kv_sub(t, 0)
                ev_ab_n = ln_chain_a(t + 1) if t + 1 < NT_A else None
                kv_sub(t, 1)
                kv_sub(t, 2)
                if ev_ab_n is not None:
                    ln_finish_a(t + 1, ev_ab_n)
                kv_sub(t, 3)
                if t == NT_A - 3:
                    for i in range(2):
                        sg.dma(nc.gpsimd,
                            AP(w1, i * 4 * 1024,
                               [[CCH * 2048, 128], [1024, 4], [1, 1024]]),
                            AP(wqkvT, i * 4 * 128 * 3072,
                               [[3072, 128], [128 * 3072, 4], [1, 1024]]))
                    for i in range(2):
                        sg.dma(nc.gpsimd,
                            AP(w1, CCH * 1024 + i * 4 * 1024,
                               [[CCH * 2048, 128], [1024, 4], [1, 1024]]),
                            AP(wprojT, i * 4 * 128 * 1024,
                               [[1024, 128], [128 * 1024, 4], [1, 1024]]))
                    ev_wB_g = sg.n

            ev_vk_done = ev_vk_sub[(NT_A - 1, 3)]
            ev_wB = ev_wB_g

            nc.scalar.wait_ge(st.sem, ev_vk_done)
            for gg in range(8):
                li = nc.scalar.copy(
                    vk_sb[:, gg * 132:(gg + 1) * 132],
                    ps[gg // 3][:, (gg % 3) * 132:(gg % 3) * 132 + 132])
                ev_vkevac = sc.inc(li)

            nc.vector.memset(vkdiag[:, :], 0.0)
            nc.vector.memset(ks_st[:, :], 0.0)
            vguard()
            nc.vector.wait_ge(sc.sem, ev_vkevac)
            for gg in range(8):
                for hl in range(4):
                    r0 = 32 * hl
                    sv.inc(nc.vector.tensor_copy(
                        vkdiag[r0:r0 + 32, gg * 128 + r0:gg * 128 + r0 + 32],
                        vk_sb[r0:r0 + 32, gg * 132 + 33 * hl:
                              gg * 132 + 33 * hl + 32]))
                    li = nc.vector.tensor_copy(
                        ks_st[r0:r0 + 32, gg * 128 + 4 * gg + hl:
                              gg * 128 + 4 * gg + hl + 1],
                        vk_sb[r0:r0 + 32, gg * 132 + 33 * hl + 32:
                              gg * 132 + 33 * hl + 33])
                    ev_stationaries = sv.inc(li)

            dump("vk_sb", vk_sb[:, :], (128, 8 * 132))
            dump("vkdiag", vkdiag[:, :], (128, 8 * 128), BF16)
            dump("ks_st", ks_st[:, :], (128, 8 * 128), BF16)
            dump("xlnA", xln2[1][:, :], (128, 4096), BF16)
            dump("k_bf", k_bf2[1][:, :], (128, 4096), BF16)
            dump("v33", v33_2[1][:, :], (128, 4224), BF16)

            if stop_after == "A":
                nc.sync.wait_ge(sv.sem, sv.n)
                sy.dma(nc.sync, outT[0:128, 0:128], cf32[:, :])
                nc.sync.wait_ge(sy.sem, sy.n)
                return nc, dbg_specs

        acc = Ecm(nc.sbuf_tensor("acc", [128, NT * 4096], F32))
        y2 = [Ecm(nc.sbuf_tensor(f"y2_{i}", [128, 4096], BF16)) for i in range(2)]
        up_bf = w1
        gt_bf = [Ecm(nc.sbuf_tensor(f"gt{i}", [128, 512], BF16)) for i in range(2)]
        with ExitStack() as bctx:
            Eb = bctx.enter_context
            x_bfB = Eb(nc.sbuf_tensor("x_bfB", [128, 4096], BF16))
            nc.gpsimd.wait_ge(st.sem, st.n)
            sg.dma(nc.gpsimd,
                AP(x_bfB, 0, [[4096, 128], [512, CCH], [1, 512]]),
                AP(xT, 0, [[N, 128], [128 * N, CCH], [1, 512]]))
            ev_xbfB0 = sg.n
            phase_barrier()
            xf32h = Eb(nc.sbuf_tensor("xf32h", [128, 1024], F32))
            xsqB = Eb(nc.sbuf_tensor("xsqB", [128, 4096], BF16))
            xlnB = Eb(nc.sbuf_tensor("xlnB", [128, 4096], BF16))
            q2 = [Eb(nc.sbuf_tensor(f"q2_{i}", [128, 4096], BF16)) for i in range(2)]
            x2bfB = Eb(nc.sbuf_tensor("x2bfB", [128, 4096], BF16))
            rn_bf = Eb(nc.sbuf_tensor("rn_bf", [32, 512], BF16))
            rn_sb = Eb(nc.sbuf_tensor("rn_sb", [32, 512], F32))
            qp_bf = [Eb(nc.sbuf_tensor(f"qp{i}", [128, 512], BF16)) for i in range(2)]
            lg_sb = Eb(nc.sbuf_tensor("lg_sb", [8, 512], F32))
            gl_sb = Eb(nc.sbuf_tensor("gl_sb", [128, 32], F32))
            gcmp = Eb(nc.sbuf_tensor("gcmp", [128, 32], F32))
            grank = Eb(nc.sbuf_tensor("grank", [128, 32], F32))
            gmask = Eb(nc.sbuf_tensor("gmask", [128, 32], F32))
            gex = Eb(nc.sbuf_tensor("gex", [128, 32], F32))
            gred = Eb(nc.sbuf_tensor("gred", [128, 4], F32))
            gtile2 = [Eb(nc.sbuf_tensor(f"gtile{i}", [128, 32], BF16)) for i in range(2)]
            attn_bf = Eb(nc.sbuf_tensor("attn_bf", [128, 4096], BF16))
            gh2 = [_PairBuf(attn_bf, None, 4096), None]

            EB = {}
            EB[("xbf", 0)] = ev_xbfB0

            def s1_stats(t):
                nc.vector.wait_ge(sg.sem, EB[("xbf", t)])
                nc.vector.wait_ge(st.sem, EB.get(("stats2mm", t - 3), 0))
                li = nc.vector.tensor_tensor(xsqB[:, :], x_bfB[:, :],
                                             x_bfB[:, :], ALU.mult)
                EB[("sq1", t)] = sv.inc(li)
                if seqb is True or 'post_sq1' in seqb:
                    phase_barrier()
                EB[("ab1", t)] = ln_stats(x_bfB, xsqB, EB[("sq1", t)],
                                          ps[3], ps[4], half=0)
                EB[("stats1mm", t)] = st.n

            def s2a_attn(t):
                nc.sync.wait_ge(sv.sem, EB.get(("resid", t - 1), 0))
                sy.dma(nc.sync,
                    AP(xf32h, 0, [[1024, 128], [512, 2], [1, 512]]),
                    AP(xT, t * 512, [[N, 128], [128 * N, 2], [1, 512]]))
                EB[("xf0", t)] = sy.n
                tbar("norm")
                nc.tensor.wait_ge(sc.sem, EB[("qdone", t)])
                if t == 0:
                    nc.tensor.wait_ge(sv.sem, ev_stationaries)
                bank_w(ps[0])
                for cc in range(CCH):
                    li = nc.tensor.matmul(ps[0][0:128, 0:512],
                                          ks_st[:, cc * 128:(cc + 1) * 128],
                                          q2[t % 2][:, cc * 512:(cc + 1) * 512],
                                          start=(cc == 0), stop=(cc == CCH - 1))
                ev_n = st.inc(li)
                nc.vector.wait_ge(st.sem, ev_n)
                nc.vector.tensor_scalar(rn_sb[:, :], ps[0][0:32, 0:512],
                                        EPS_ATTN, None, ALU.add)
                vguard()
                nc.vector.reciprocal(rn_sb[:, :], rn_sb[:, :])
                vguard()
                li = nc.vector.tensor_copy(rn_bf[:, :], rn_sb[:, :])
                ev_rn = sv.inc(li)
                bank_r(ps[0], sv, ev_rn)
                tbar("attn")
                nc.tensor.wait_ge(sv.sem, ev_rn)
                for cc in range(CCH):
                    bkR = ps[1 + cc % 2]
                    bkM = ps[3 + cc % 2]
                    bank_w(bkR)
                    li = nc.tensor.matmul(
                        bkR[:, 0:512],
                        cbf[0:32, CB_SELH + cc * 128:CB_SELH + (cc + 1) * 128],
                        rn_bf[:, :], start=True, stop=True)
                    evR = st.inc(li)
                    nc.vector.wait_ge(st.sem, evR)
                    li = nc.vector.tensor_tensor(qp_bf[cc % 2][:, :],
                                                 q2[t % 2][:, cc * 512:(cc + 1) * 512],
                                                 bkR[:, 0:512], ALU.mult)
                    ev_qp = sv.inc(li)
                    bank_r(bkR, sv, ev_qp)
                    nc.tensor.wait_ge(sv.sem, ev_qp)
                    bank_w(bkM)
                    li = nc.tensor.matmul(bkM[:, 0:512],
                                          vkdiag[:, cc * 128:(cc + 1) * 128],
                                          qp_bf[cc % 2][:, :], start=True,
                                          stop=True)
                    evM = st.inc(li)
                    nc.scalar.wait_ge(st.sem, evM)
                    li = nc.scalar.copy(attn_bf[:, cc * 512:(cc + 1) * 512],
                                        bkM[:, 0:512])
                    EB[("attnevac", t)] = sc.inc(li)
                    bank_r(bkM, sc, EB[("attnevac", t)])

            def s2b_prep(t):
                nc.scalar.wait_ge(sv.sem, EB[("resid", t)])
                nc.scalar.wait_ge(st.sem, EB.get(("stats2mm", t - 1), 0))
                nc.scalar.wait_ge(sv.sem, EB.get(("y", t - 1), 0))
                li = nc.scalar.copy(x2bfB[:, :], acc[:, t * 4096:(t + 1) * 4096])
                EB[("x2bf", t)] = sc.inc(li)
                nc.vector.wait_ge(sc.sem, EB[("x2bf", t)])
                nc.vector.wait_ge(st.sem, EB[("stats1mm", t + 2)] if
                                  (("stats1mm", t + 2) in EB) else 0)
                li = nc.vector.tensor_tensor(xsqB[:, :], x2bfB[:, :],
                                             x2bfB[:, :], ALU.mult)
                EB[("sq2", t)] = sv.inc(li)
            def s2a_proj(t):
                tbar("proj")
                nc.tensor.wait_ge(sc.sem, EB[("attnevac", t)])
                for oc in range(CCH):
                    bk = ps[5 + oc % 2]
                    bank_w(bk)
                    for cc in range(CCH):
                        li = nc.tensor.matmul(
                            bk[:, 0:512],
                            w1[:, CCH * 1024 + cc * 1024 + oc * 128:
                               CCH * 1024 + cc * 1024 + (oc + 1) * 128],
                            attn_bf[:, cc * 512:(cc + 1) * 512],
                            start=(cc == 0), stop=(cc == CCH - 1))
                    evt = st.inc(li)
                    nc.scalar.wait_ge(st.sem, evt)
                    li = nc.scalar.activation(
                        acc[:, t * 4096 + oc * 512:t * 4096 + (oc + 1) * 512],
                        bk[:, 0:512], AF.Identity, bias=bproj[:, oc:oc + 1])
                    EB[("pj", t, oc)] = sc.inc(li)
                    bank_r(bk, sc, EB[("pj", t, oc)])
                EB[("projmm", t)] = st.n
                ev_q = EB[("xf0", t)]
                for qq in range(4):
                    nc.vector.wait_ge(sc.sem, EB[("pj", t, 2 * qq + 1)])
                    nc.vector.wait_ge(sy.sem, ev_q)
                    for oc in (2 * qq, 2 * qq + 1):
                        li = nc.vector.tensor_tensor(
                            acc[:, t * 4096 + oc * 512:t * 4096 + (oc + 1) * 512],
                            acc[:, t * 4096 + oc * 512:t * 4096 + (oc + 1) * 512],
                            xf32h[:, (oc % 2) * 512:(oc % 2 + 1) * 512], ALU.add)
                        ev_r = sv.inc(li)
                    if qq < 3:
                        nc.sync.wait_ge(sv.sem, ev_r)
                        sy.dma(nc.sync,
                            AP(xf32h, 0, [[1024, 128], [512, 2], [1, 512]]),
                            AP(xT, t * 512 + (2 * qq + 2) * 128 * N,
                               [[N, 128], [128 * N, 2], [1, 512]]))
                        ev_q = sy.n
                        sy_n_chk = sy.n
                EB[("resid", t)] = ev_r

            def s1_finish(t):
                tbar("bcast1")
                EB[("abc1", t)] = ln_bcast(EB[("ab1", t)], ps[3], ps[4],
                                           None, None, half=0)
                nc.vector.wait_ge(st.sem, EB.get(("qmm", t - 1), 0))
                EB[("xln1", t)] = ln_apply(x_bfB, ps[3][:, 0:512],
                                           ps[4][:, 0:512], xlnB,
                                           EB[("abc1", t)])
                if t + 1 < NT:
                    nc.gpsimd.wait_ge(sv.sem, EB[("xln1", t)])
                    sg.dma(nc.gpsimd,
                        AP(x_bfB, 0, [[4096, 128], [512, CCH], [1, 512]]),
                        AP(xT, (t + 1) * 512, [[N, 128], [128 * N, CCH], [1, 512]]))
                    EB[("xbf", t + 1)] = sg.n
                bank_r(ps[3], sv, EB[("xln1", t)])
                bank_r(ps[4], sv, EB[("xln1", t)])

            def s1_qproj(t):
                tbar("q")
                nc.tensor.wait_ge(sv.sem, EB[("xln1", t)])
                if t == 0:
                    nc.tensor.wait_ge(sg.sem, ev_wB)
                for oc in range(CCH):
                    bk = ps[5 + oc % 2]
                    bank_w(bk)
                    for cc in range(CCH):
                        li = nc.tensor.matmul(
                            bk[:, 0:512],
                            w1[:, cc * 1024 + oc * 128:cc * 1024 + (oc + 1) * 128],
                            xlnB[:, cc * 512:(cc + 1) * 512],
                            start=(cc == 0), stop=(cc == CCH - 1))
                    evt = st.inc(li)
                    nc.scalar.wait_ge(st.sem, evt)
                    li = nc.scalar.activation(
                        q2[t % 2][:, oc * 512:(oc + 1) * 512],
                        bk[:, 0:512], AF.Relu)
                    ev_qe = sc.inc(li)
                    bank_r(bk, sc, ev_qe)
                EB[("qdone", t)] = ev_qe
                EB[("qmm", t)] = st.n

            def s2b_stats(t):
                tbar("stats2")
                EB[("ab2", t)] = ln_stats(x2bfB, xsqB, EB[("sq2", t)],
                                          ps[7], ps[5], half=0)
                EB[("stats2mm", t)] = st.n
            def s2b_finish(t):
                EB[("abc2", t)] = ln_bcast(EB[("ab2", t)], ps[1], ps[2],
                                           None, None, half=0)
                nc.vector.wait_ge(sy.sem, EB.get(("yst", t - 2), 0))
                nc.vector.wait_ge(st.sem, EB.get(("routermm", t - 2), 0))
                EB[("y", t)] = ln_apply(x2bfB, ps[1][:, 0:512],
                                        ps[2][:, 0:512], y2[t % 2],
                                        EB[("abc2", t)])
                bank_r(ps[1], sv, EB[("y", t)])
                bank_r(ps[2], sv, EB[("y", t)])
                nc.sync.wait_ge(sv.sem, EB[("y", t)])
                sy.dma(nc.sync,
                    AP(yscr, t * 512, [[TOK, 128], [128 * TOK, CCH], [1, 512]]),
                    AP(y2[t % 2], 0, [[4096, 128], [512, CCH], [1, 512]]))
                EB[("yst", t)] = sy.n
            def s2c_router(t):
                tbar("router")
                nc.tensor.wait_ge(sv.sem, EB[("y", t)])
                if t == 0:
                    nc.tensor.wait_ge(sg.sem, ev_rw)
                bank_w(ps[7])
                for cc in range(CCH):
                    li = nc.tensor.matmul(ps[7][0:8, 0:512],
                                          rw[:, cc * 8:(cc + 1) * 8],
                                          y2[t % 2][:, cc * 512:(cc + 1) * 512],
                                          start=(cc == 0), stop=(cc == CCH - 1))
                ev_lg = st.inc(li)
                EB[("routermm", t)] = ev_lg
                nc.scalar.wait_ge(st.sem, ev_lg)
                li = nc.scalar.copy(lg_sb[:, :], ps[7][0:8, 0:512])
                ev_lge = sc.inc(li)
                bank_r(ps[7], sc, ev_lge)
                nc.tensor.wait_ge(sc.sem, ev_lge)
                bank_w(ps[0])
                for sub in range(4):
                    li = nc.tensor.matmul(ps[0][0:128, sub * 8:(sub + 1) * 8],
                                          lg_sb[0:8, sub * 128:(sub + 1) * 128],
                                          cf32[0:8, 0:8], start=True, stop=True)
                ev_tp = st.inc(li)
                nc.vector.wait_ge(st.sem, ev_tp)
                li = nc.vector.tensor_copy(gl_sb[:, :], ps[0][0:128, 0:32])
                ev_gl = sv.inc(li)
                bank_r(ps[0], sv, ev_gl)
                nc.vector.wait_ge(sv.sem, ev_gl)
                nc.vector.memset(grank[:, :], 0.0)
                vguard()
                for j in range(E):
                    for sub in range(4):
                        nc.vector.tensor_scalar(
                            gcmp[:, sub * 8:(sub + 1) * 8],
                            gl_sb[:, sub * 8:(sub + 1) * 8],
                            gl_sb[:, sub * 8 + j:sub * 8 + j + 1], None,
                            ALU.is_lt)
                    vguard()
                    for sub in range(4):
                        nc.vector.tensor_tensor(
                            grank[:, sub * 8:(sub + 1) * 8],
                            grank[:, sub * 8:(sub + 1) * 8],
                            gcmp[:, sub * 8:(sub + 1) * 8], ALU.add)
                    vguard()
                nc.vector.tensor_scalar(gmask[:, :], grank[:, :], 3.5, None,
                                        ALU.is_lt)
                for sub in range(4):
                    li = nc.vector.reduce_max(gred[:, sub:sub + 1],
                                              gl_sb[:, sub * 8:(sub + 1) * 8],
                                              axis=AXX, negate=True)
                    ev_gm = sv.inc(li)
                nc.scalar.wait_ge(sv.sem, ev_gm)
                for sub in range(4):
                    li = nc.scalar.activation(gex[:, sub * 8:(sub + 1) * 8],
                                              gl_sb[:, sub * 8:(sub + 1) * 8],
                                              AF.Exp, bias=gred[:, sub:sub + 1])
                ev_gex = sc.inc(li)
                nc.vector.wait_ge(sc.sem, ev_gex)
                nc.vector.tensor_tensor(gex[:, :], gex[:, :], gmask[:, :],
                                        ALU.mult)
                vguard()
                for sub in range(4):
                    nc.vector.reduce_sum(gred[:, sub:sub + 1],
                                         gex[:, sub * 8:(sub + 1) * 8], axis=AXX)
                vguard()
                nc.vector.reciprocal(gred[:, :], gred[:, :])
                vguard()
                for sub in range(4):
                    li = nc.vector.tensor_scalar(
                        gtile2[t % 2][:, sub * 8:(sub + 1) * 8],
                        gex[:, sub * 8:(sub + 1) * 8],
                        gred[:, sub:sub + 1], None, ALU.mult)
                    EB[("gt", t)] = sv.inc(li)
            def s2c_gtp(t):
                nc.tensor.wait_ge(sv.sem, EB[("gt", t)])
                bank_w(ps[2])
                for sub in range(4):
                    li = nc.tensor.matmul(ps[2][0:8, sub * 128:(sub + 1) * 128],
                                          gtile2[t % 2][:, sub * 8:(sub + 1) * 8],
                                          ident_bf[:, 0:128], start=True,
                                          stop=True)
                ev_gtp = st.inc(li)
                nc.scalar.wait_ge(st.sem, ev_gtp)
                li = nc.scalar.copy(gates_fm[0:8, t * 512:(t + 1) * 512],
                                    ps[2][0:8, 0:512])
                EB[("gfm", t)] = sc.inc(li)
                bank_r(ps[2], sc, EB[("gfm", t)])

            ev_up_used, ev_dn_used, EB_C = {}, {}, {}
            ev_gelu, ev_gh, ev_acc = {}, {}, {}

            def c_gate(s, t):
                e = s // 2
                if s == 0:
                    nc.tensor.wait_ge(sc.sem, EB[("gfm", t)])
                bank_w(ps[4 + t % 2])
                li = nc.tensor.matmul(
                    ps[4 + t % 2][:, 0:512],
                    cbf[0:8, CB_SELG + e * 128:CB_SELG + (e + 1) * 128],
                    gates_fm[0:8, t * 512:(t + 1) * 512],
                    start=True, stop=True)
                EB_C[("gmm", t)] = st.inc(li)

            def c_up(s, t, first=False, hcs=None):
                hcs = range(HCH) if hcs is None else hcs
                ysrc = lambda cc: y2[t % 2][:, cc * 512:(cc + 1) * 512]
                if s == 0 and t >= 2 and hcs[0] == 0:
                    nc.tensor.wait_ge(sv.sem, EB[("y", t)])
                for hc in hcs:
                    if first and hc % 4 == 0:
                        nc.tensor.wait_ge(sg.sem, EB_C["up_parts"][hc // 4])
                    bk = ps[hc % 2]
                    bank_w(bk)
                    pv = (s, t, hc - 2) if hc >= 2 else (
                        (s, t - 1, HCH - 2 + hc) if t >= 1 else
                        (s - 1, NT - 1, HCH - 2 + hc))
                    if pv in ev_gelu:
                        nc.tensor.wait_ge(sc.sem, ev_gelu[pv])
                    for cc in range(CCH):
                        li = nc.tensor.matmul(
                            bk[:, 0:512],
                            up_bf[:, cc * 2048 + hc * 128:
                                  cc * 2048 + (hc + 1) * 128],
                            ysrc(cc),
                            start=(cc == 0), stop=(cc == CCH - 1))
                    evu = st.inc(li)
                    nc.scalar.wait_ge(st.sem, evu)
                    if pv in ev_gh:
                        nc.scalar.wait_ge(sv.sem, ev_gh[pv])
                    li = nc.scalar.activation(gt_bf[hc % 2][:, :], bk[:, 0:512],
                                              AF.Gelu_apprx_tanh)
                    ev_gelu[(s, t, hc)] = sc.inc(li)
                    nc.vector.wait_ge(sc.sem, ev_gelu[(s, t, hc)])
                    if hc == 0:
                        nc.vector.wait_ge(st.sem, EB_C[("gmm", t)])
                        nc.vector.wait_ge(st.sem, EB_C.get(("ghfree", t % 2), 0))
                    li = nc.vector.tensor_tensor(
                        gh2[t % 2][:, hc * 512:(hc + 1) * 512],
                        gt_bf[hc % 2][:, :], ps[4 + t % 2][:, 0:512], ALU.mult)
                    ev_gh[(s, t, hc)] = sv.inc(li)
                bank_r(ps[4 + t % 2], sv, ev_gh[(s, t, hcs[-1])])
                ev_up_used[s] = st.n

            def c_down(s, t):
                nc.tensor.wait_ge(sv.sem, ev_gh[(s, t, HCH - 1)])
                for oc in range(CCH):
                    if EB_C.pop("dn_first_pending", None) == (s,) and False:
                        pass
                    if EB_C.get("dn_first") == (s,) and oc % 2 == 0:
                        nc.tensor.wait_ge(sg.sem, EB_C["dn_parts"][oc // 2])
                    bk = ps[2 + oc % 2]
                    bank_w(bk)
                    pv = (s, t, oc - 2) if oc >= 2 else (
                        (s, t - 1, CCH - 2 + oc) if t >= 1 else
                        (s - 1, NT - 1, CCH - 2 + oc))
                    if pv in ev_acc:
                        nc.tensor.wait_ge(sv.sem, ev_acc[pv])
                    for hc in range(HCH):
                        li = nc.tensor.matmul(
                            bk[:, 0:512],
                            dn_bf[:, hc * 1024 + oc * 128:
                                  hc * 1024 + (oc + 1) * 128],
                            gh2[t % 2][:, hc * 512:(hc + 1) * 512],
                            start=(hc == 0), stop=(hc == HCH - 1))
                    evd = st.inc(li)
                    nc.vector.wait_ge(st.sem, evd)
                    li = nc.vector.tensor_tensor(
                        acc[:, t * 4096 + oc * 512:t * 4096 + (oc + 1) * 512],
                        acc[:, t * 4096 + oc * 512:t * 4096 + (oc + 1) * 512],
                        bk[:, 0:512], ALU.add)
                    ev_acc[(s, t, oc)] = sv.inc(li)
                    bank_r(bk, sv, ev_acc[(s, t, oc)])
                    if s == 2 * E - 1:
                        nc.gpsimd.wait_ge(sv.sem, ev_acc[(s, t, oc)])
                        sg.dma(nc.gpsimd,
                            AP(outT, t * 512 + oc * 128 * TOK,
                               [[TOK, 128], [1, 512]]),
                            AP(acc, t * 4096 + oc * 512,
                               [[NT * 4096, 128], [1, 512]]))
                EB_C[("ghfree", t % 2)] = st.n
                ev_dn_used[s] = st.n

            seqb = getattr(build_program, "SEQB", ())

            def maybe_bar(name):
                if seqb is True or name in seqb:
                    phase_barrier()
                mode = getattr(build_program, "BARMODE", None)
                if mode and name == "s1_stats":
                    marks = {"T": (st, st.n), "V": (sv, sv.n), "S": (sc, sc.n),
                             "G": (sg, sg.n), "Y": (sy, sy.n)}
                    engs = {"t": nc.tensor, "v": nc.vector, "s": nc.scalar,
                            "g": nc.gpsimd, "y": nc.sync}
                    for pair in mode.split(","):
                        e, cs = pair.split(":")
                        for ch in cs:
                            ctr, val = marks[ch]
                            engs[e].wait_ge(ctr.sem, val)

            LAGS = (0, 1, 2, 1, 0, 3, 2, 4, 2)
            lg_s1s, lg_attn, lg_prep, lg_proj, lg_s1q, lg_rt, lg_s2bs, lg_gtp, lg_fin = LAGS
            for i in range(NT + max(LAGS)):
                if 0 <= i - lg_s1s < NT:
                    maybe_bar('pre_s1'); s1_stats(i - lg_s1s); maybe_bar('s1_stats')
                if 0 <= i - lg_attn < NT:
                    s2a_attn(i - lg_attn); maybe_bar('s2a_attn')
                if 0 <= i - lg_s1q < NT:
                    s1_finish(i - lg_s1q)
                if 0 <= i - lg_prep < NT:
                    s2b_prep(i - lg_prep); maybe_bar('s2b_prep')
                if 0 <= i - lg_proj < NT:
                    s2a_proj(i - lg_proj); maybe_bar('s2a_proj')
                if 0 <= i - lg_s1q < NT:
                    s1_qproj(i - lg_s1q); maybe_bar('s1_apply_qproj')
                if 0 <= i - lg_gtp < NT - 1:
                    s2c_gtp(i - lg_gtp); maybe_bar('s2c_gtp')
                if i == 6:
                    c_gate(0, 2)
                    c_up(0, 2, first=True, hcs=range(8))
                if 0 <= i - lg_rt < NT:
                    s2c_router(i - lg_rt); maybe_bar('s2c_router')
                if 0 <= i - lg_s2bs < NT:
                    s2b_stats(i - lg_s2bs); maybe_bar('s2b_stats')
                if 0 <= i - lg_fin < NT:
                    s2b_finish(i - lg_fin); maybe_bar('s2b_finish')
                if i == 5:
                    nc.gpsimd.wait_ge(st.sem, EB[("projmm", NT - 1)])
                    up_parts0 = []
                    for j in range(4):
                        sg.dma(nc.gpsimd,
                            AP(up_bf, j * 512,
                               [[CCH * 2048, 128], [2048, CCH], [1, 512]]),
                            AP(upw, j * 512,
                               [[Hf, 128], [128 * Hf, CCH], [1, 512]]))
                        up_parts0.append(sg.n)
                    EB_C["up_parts"] = up_parts0

            ev_y_all_st = EB[("yst", NT - 1)]
            ev_projmm_last = EB[("projmm", NT - 1)]
            ev_stats2mm_last = EB[("stats2mm", NT - 1)]
            ev_y_last = EB[("y", NT - 1)]
            ev_router_last = EB[("routermm", NT - 1)]
            dump("gates_fm", gates_fm[:, :], (8, TOK), BF16)
            dump("accB", acc[:, :], (128, NT * 4096))

            if stop_after == "B":
                nc.sync.wait_ge(sc.sem, sc.n)
                nc.sync.wait_ge(sv.sem, sv.n)
                sy.dma(nc.sync, outT[0:128, 0:128], cf32[:, :])
                nc.sync.wait_ge(sy.sem, sy.n)
                return nc, dbg_specs

        ev_b_end = {"st": st.n, "sv": sv.n, "sc": sc.n, "sy": sy.n, "sg": sg.n}
        with ExitStack() as cctx:
            Ec = cctx.enter_context
            dn_bf = Ec(nc.sbuf_tensor("dn_bf", [128, HCH * 1024], BF16))
            gh2[1] = Ec(nc.sbuf_tensor("gh1", [128, HCH * 512], BF16))
            gh2[0].hi = Ec(nc.sbuf_tensor("gh0hi", [128, 4096], BF16))

            for s in range(2 * E):
                e, hh = s // 2, s % 2
                if s >= 1:
                    nc.gpsimd.wait_ge(st.sem, ev_up_used[s - 1])
                    up_parts = []
                    for i in range(4):
                        sg.dma(nc.gpsimd,
                            AP(up_bf, i * 512,
                               [[CCH * 2048, 128], [2048, CCH], [1, 512]]),
                            AP(upw, e * C * Hf + hh * 2048 + i * 512,
                               [[Hf, 128], [128 * Hf, CCH], [1, 512]]))
                        up_parts.append(sg.n)
                else:
                    up_parts = EB_C["up_parts"]
                if s >= 1:
                    nc.gpsimd.wait_ge(st.sem, ev_dn_used[s - 1])
                else:
                    nc.gpsimd.wait_ge(st.sem, ev_stats2mm_last)
                dn_parts = []
                for i in range(4):
                    sg.dma(nc.gpsimd,
                        AP(dn_bf, i * 256,
                           [[HCH * 1024, 128], [1024, HCH], [1, 256]]),
                        AP(downw, e * Hf * C + hh * 2048 * C + i * 256,
                           [[C, 128], [128 * C, HCH], [1, 256]]))
                    dn_parts.append(sg.n)
                EB_C["up_parts"], EB_C["dn_parts"] = up_parts, dn_parts
                if s == 0:
                    EB_C["resident"] = 1
                res = EB_C["resident"]
                firstseg = True
                for k, tp2 in enumerate([res, 1 - res]):
                    t0, t1 = 2 * tp2, 2 * tp2 + 1
                    if k == 1:
                        if EB_C.get("first_ys", True):
                            nc.sync.wait_ge(sy.sem, ev_y_all_st)
                            EB_C["first_ys"] = False
                        nc.sync.wait_ge(st.sem, EB_C["yfree"])
                        for j in (t0, t1):
                            sy.dma(nc.sync,
                                AP(y2[j % 2], 0, [[4096, 128], [512, CCH], [1, 512]]),
                                AP(yscr, j * 512, [[TOK, 128], [128 * TOK, CCH], [1, 512]]))
                        nc.tensor.wait_ge(sy.sem, sy.n)
                        EB_C["resident"] = 1 - res
                    if firstseg:
                        EB_C["dn_first"] = (s,)
                    if s == 0 and t0 == 2:
                        c_up(0, 2, first=True, hcs=range(8, HCH))
                        s2c_gtp(NT - 1)
                        c_down(s, t0)
                        EB_C.pop("dn_first", None)
                        c_gate(s, t1)
                        c_up(s, t1)
                        EB_C["yfree"] = st.n
                        c_down(s, t1)
                        firstseg = False
                        continue
                    c_gate(s, t0)
                    c_up(s, t0, first=firstseg)
                    c_gate(s, t1)
                    c_up(s, t1)
                    if k == 0:
                        EB_C["yfree"] = st.n
                    c_down(s, t0)
                    EB_C.pop("dn_first", None)
                    c_down(s, t1)
                    firstseg = False


        nc.sync.wait_ge(sg.sem, sg.n)
        nc.sync.wait_ge(sy.sem, sy.n)

    return nc, dbg_specs



def _prep_inputs(L0_attn_x, L0_attn_W_qkv, L0_attn_W_proj, L0_attn_b_proj,
                 L0_ffn_router_w, L0_ffn_up_w, L0_ffn_down_w):
    x = np.asarray(L0_attn_x, dtype=np.float32)
    wqkvT = np.ascontiguousarray(np.asarray(L0_attn_W_qkv, np.float32).T)
    wprojT = np.ascontiguousarray(np.asarray(L0_attn_W_proj, np.float32).T)
    bprojH = np.ascontiguousarray(
        np.asarray(L0_attn_b_proj, np.float32).reshape(8, 128).T)
    rwf = np.asarray(L0_ffn_router_w, np.float32)
    rwH = np.ascontiguousarray(
        rwf.reshape(8, 128, 8).transpose(1, 0, 2).reshape(128, 64))
    upwf = np.ascontiguousarray(np.asarray(L0_ffn_up_w, np.float32))
    downwf = np.ascontiguousarray(np.asarray(L0_ffn_down_w, np.float32))
    cbf, cf32 = host_consts()
    in_maps = []
    for core in range(8):
        b, half = core // 2, core % 2
        xTb = np.ascontiguousarray(x[b].T)
        if half:
            xTb = np.ascontiguousarray(np.roll(xTb, -TOK, axis=1))
        in_maps.append({
            "xT": xTb, "wqkvT": wqkvT, "wprojT": wprojT, "bprojH": bprojH,
            "rwH": rwH, "upw": upwf, "downw": downwf, "cbfH": cbf,
            "cf32H": cf32,
        })
    return in_maps


def kernel(L0_attn_x, L0_attn_W_qkv, L0_attn_W_proj, L0_attn_b_proj,
           L0_ffn_router_w, L0_ffn_up_w, L0_ffn_down_w):
    in_maps = _prep_inputs(L0_attn_x, L0_attn_W_qkv, L0_attn_W_proj,
                           L0_attn_b_proj, L0_ffn_router_w, L0_ffn_up_w,
                           L0_ffn_down_w)
    nc, _ = build_program()
    res = run_bass_kernel_spmd(nc, in_maps, core_ids=list(range(8)))
    out = np.empty((B, N, C), dtype=np.float32)
    for core in range(8):
        b, half = core // 2, core % 2
        out[b, half * TOK:(half + 1) * TOK, :] = res.results[core]["outT"].T
    return out


# version 13 (final)
# speedup vs baseline: 1.0074x; 1.0074x over previous
import numpy as np
import ml_dtypes
from contextlib import ExitStack


class _PairBuf:
    def __init__(self, lo, hi, half):
        self.lo, self.hi, self.half = lo, hi, half

    def __getitem__(self, idx):
        p, c = idx
        if c.start >= self.half:
            return self.hi[p, c.start - self.half:c.stop - self.half]
        assert c.stop <= self.half
        return self.lo[p, c.start:c.stop]

import concourse.bass as bass
import concourse.mybir as mybir
from concourse.bass_utils import run_bass_kernel_spmd

F32 = mybir.dt.float32
BF16 = mybir.dt.bfloat16
AF = mybir.ActivationFunctionType
ALU = mybir.AluOpType
AXX = mybir.AxisListType.X
BFNP = ml_dtypes.bfloat16
AP = bass.AP

B, N, C = 4, 4096, 1024
E, Hf = 8, 4096
TOK, TILE = 2048, 512
NT_A, NT = 8, 4
CCH = 8
HCH = 16
EPS_ATTN, EPS_LN = 1e-15, 1e-5

CB_ID, CB_ONES, CB_SELH = 0, 128, 256
CB_SELG = CB_SELH + 8 * 128
CBW = CB_SELG + 8 * 128


def host_consts():
    cbf = np.zeros((128, CBW), dtype=np.float32)
    cbf[:, CB_ID:CB_ID + 128] = np.eye(128)
    cbf[:, CB_ONES:CB_ONES + 128] = 1.0
    for c in range(CCH):
        blk = np.zeros((128, 128), dtype=np.float32)
        for hl in range(4):
            blk[4 * c + hl, 32 * hl:32 * hl + 32] = 1.0
        cbf[:, CB_SELH + 128 * c:CB_SELH + 128 * (c + 1)] = blk
    for e in range(E):
        blk = np.zeros((128, 128), dtype=np.float32)
        blk[e, :] = 1.0
        cbf[:, CB_SELG + 128 * e:CB_SELG + 128 * (e + 1)] = blk
    return cbf.astype(BFNP), np.eye(128, dtype=np.float32)


class Ctr:
    def __init__(self, ctx, nc, name):
        self.sem = ctx.enter_context(nc.semaphore(name))
        self.n = 0

    def inc(self, instr, k=1):
        instr.then_inc(self.sem, k)
        self.n += k
        return self.n

    def dma(self, eng, out, in_):
        eng.wait_ge(self.sem, self.n)
        eng.dma_start(out, in_).then_inc(self.sem, 16)
        self.n += 16
        return self.n


def build_program(stop_after=None, debug=()):
    nc = bass.Bass("TRN2", target_bir_lowering=False)
    dbg_specs = []

    xT = nc.dram_tensor("xT", [C, N], F32, kind="ExternalInput")
    wqkvT = nc.dram_tensor("wqkvT", [C, 3 * C], F32, kind="ExternalInput")
    wprojT = nc.dram_tensor("wprojT", [C, C], F32, kind="ExternalInput")
    bprojH = nc.dram_tensor("bprojH", [128, 8], F32, kind="ExternalInput")
    rwH = nc.dram_tensor("rwH", [128, 8 * E], F32, kind="ExternalInput")
    upw = nc.dram_tensor("upw", [E, C, Hf], F32, kind="ExternalInput")
    downw = nc.dram_tensor("downw", [E, Hf, C], F32, kind="ExternalInput")
    cbfH = nc.dram_tensor("cbfH", [128, CBW], BF16, kind="ExternalInput")
    cf32H = nc.dram_tensor("cf32H", [128, 128], F32, kind="ExternalInput")
    yscr = nc.dram_tensor("yscr", [C, TOK], BF16)
    outT = nc.dram_tensor("outT", [C, TOK], F32, kind="ExternalOutput")

    ctx = ExitStack()
    with ctx:
        Ecm = ctx.enter_context

        sg = Ctr(ctx, nc, "sg")
        sy = Ctr(ctx, nc, "sy")
        st = Ctr(ctx, nc, "st")
        sv = Ctr(ctx, nc, "sv")
        sc = Ctr(ctx, nc, "sc")

        ps = [Ecm(nc.psum_tensor(f"ps{i}", [128, 512], F32)) for i in range(8)]

        cbf = Ecm(nc.sbuf_tensor("cbf", [128, CBW], BF16))
        cf32 = Ecm(nc.sbuf_tensor("cf32", [128, 128], F32))
        bproj = Ecm(nc.sbuf_tensor("bproj", [128, 8], F32))
        epsb = Ecm(nc.sbuf_tensor("epsb", [128, 1], F32))
        rw = Ecm(nc.sbuf_tensor("rw", [128, 8 * E], BF16))
        vkdiag = Ecm(nc.sbuf_tensor("vkdiag", [128, 8 * 128], BF16))
        ks_st = Ecm(nc.sbuf_tensor("ks_st", [128, 8 * 128], BF16))
        gates_fm = Ecm(nc.sbuf_tensor("gates_fm", [8, TOK], BF16))
        u_sb = Ecm(nc.sbuf_tensor("u_sb", [1, 1024], F32))
        msq_sb = Ecm(nc.sbuf_tensor("msq_sb", [1, 512], F32))
        var_sb = Ecm(nc.sbuf_tensor("var_sb", [1, 512], F32))
        rst_sb = Ecm(nc.sbuf_tensor("rst_sb", [1, 512], F32))
        a_bf = Ecm(nc.sbuf_tensor("a_bf", [1, 512], BF16))
        b_bf = Ecm(nc.sbuf_tensor("b_bf", [1, 512], BF16))

        ident_bf = cbf[:, CB_ID:CB_ID + 128]
        ones_bf = cbf[:, CB_ONES:CB_ONES + 128]

        blk = Ecm(nc.Block())

        def tbar(pos):
            tb = getattr(build_program, "TBAR", ("norm:v",))
            if pos in tb or (pos + ":v") in tb:
                nc.tensor.wait_ge(sv.sem, sv.n)
            if pos in tb or (pos + ":s") in tb:
                nc.tensor.wait_ge(sc.sem, sc.n)

        def vguard():
            nc.vector.engine_nop().then_inc(sv.sem, 1)
            sv.n += 1
            nc.vector.wait_ge(sv.sem, sv.n)

        def dump(tag, src_ap, shape, dtype=F32):
            if tag not in debug:
                return
            d = nc.dram_tensor("dbg_" + tag, list(shape), dtype,
                               kind="ExternalOutput")
            for ctr in (st, sv, sc, sg):
                nc.sync.wait_ge(ctr.sem, ctr.n)
            sy.dma(nc.sync, d[:, :], src_ap)
            for eng in (nc.tensor, nc.vector, nc.scalar, nc.gpsimd):
                eng.wait_ge(sy.sem, sy.n)
            dbg_specs.append(tag)

        bankdep = {}

        def bank_w(*banks):
            for b in banks:
                for (sem_, ev_) in bankdep.pop(id(b), []):
                    nc.tensor.wait_ge(sem_, ev_)

        def bank_r(b, ctr, ev):
            bankdep.setdefault(id(b), []).append((ctr.sem, ev))

        ln_state = {"ln_ev": 0}
        def ln_stats(x_bf_t, xsq_t, ev_ready_val, bank_x, bank_sq, half=0,
                     ev_sc_free=None):
            h5 = half * 512
            nc.tensor.wait_ge(sv.sem, ev_ready_val)
            if ev_sc_free:
                nc.tensor.wait_ge(sc.sem, ev_sc_free)
            bank_w(bank_x, bank_sq)
            for cc in range(CCH):
                nc.tensor.matmul(bank_x[0:1, 0:512], ones_bf[:, 0:1],
                                 x_bf_t[:, cc * 512:(cc + 1) * 512],
                                 start=(cc == 0), stop=(cc == CCH - 1))
            for cc in range(CCH):
                li = nc.tensor.matmul(bank_sq[0:1, 0:512], ones_bf[:, 0:1],
                                      xsq_t[:, cc * 512:(cc + 1) * 512],
                                      start=(cc == 0), stop=(cc == CCH - 1))
            ev = st.inc(li)
            nc.vector.wait_ge(st.sem, ev)
            nc.vector.tensor_scalar(u_sb[0:1, 2 * h5:2 * h5 + 512],
                                    bank_x[0:1, 0:512],
                                    1.0 / 1024.0, None, ALU.mult)
            li = nc.vector.tensor_scalar(u_sb[0:1, 2 * h5 + 512:2 * h5 + 1024],
                                         bank_sq[0:1, 0:512],
                                         1.0 / 1024.0, None, ALU.mult)
            ev_u = sv.inc(li)
            bank_r(bank_x, sv, ev_u)
            bank_r(bank_sq, sv, ev_u)
            nc.vector.wait_ge(sv.sem, ev_u)
            nc.vector.tensor_tensor(msq_sb[0:1, h5:h5 + 512],
                                    u_sb[0:1, 2 * h5:2 * h5 + 512],
                                    u_sb[0:1, 2 * h5:2 * h5 + 512], ALU.mult)
            vguard()
            nc.vector.wait_ge(sc.sem, ln_state["ln_ev"])
            li = nc.vector.scalar_tensor_tensor(
                var_sb[0:1, h5:h5 + 512],
                u_sb[0:1, 2 * h5 + 512:2 * h5 + 1024],
                EPS_LN, msq_sb[0:1, h5:h5 + 512],
                ALU.add, ALU.subtract)
            ev_v = sv.inc(li)
            nc.scalar.wait_ge(sv.sem, ev_v)
            li = nc.scalar.activation(rst_sb[0:1, h5:h5 + 512],
                                      var_sb[0:1, h5:h5 + 512], AF.Ln)
            ln_state["ln_ev"] = sc.inc(li)
            nc.scalar.wait_ge(sc.sem, ln_state["ln_ev"])
            li = nc.scalar.activation(rst_sb[0:1, h5:h5 + 512],
                                      rst_sb[0:1, h5:h5 + 512], AF.Exp,
                                      scale=-0.5)
            ev_s = sc.inc(li)
            nc.vector.wait_ge(sc.sem, ev_s)
            li = nc.vector.tensor_copy(a_bf[0:1, h5:h5 + 512],
                                       rst_sb[0:1, h5:h5 + 512])
            sv.inc(li)
            li = nc.vector.tensor_tensor(b_bf[0:1, h5:h5 + 512],
                                         u_sb[0:1, 2 * h5:2 * h5 + 512],
                                         rst_sb[0:1, h5:h5 + 512], ALU.mult)
            return sv.inc(li)

        def ln_bcast(ev_ab, bankA, bankB, abc_bf, bbc_bf, half=0,
                     ev_banks_free_sc=None):
            h5 = half * 512
            nc.tensor.wait_ge(sv.sem, ev_ab)
            if ev_banks_free_sc:
                nc.tensor.wait_ge(sc.sem, ev_banks_free_sc)
            bank_w(bankA, bankB)
            nc.tensor.matmul(bankA[:, 0:512], ones_bf[0:1, 0:128],
                             a_bf[0:1, h5:h5 + 512], start=True, stop=True)
            li = nc.tensor.matmul(bankB[:, 0:512], ones_bf[0:1, 0:128],
                                  b_bf[0:1, h5:h5 + 512], start=True,
                                  stop=True)
            ev = st.inc(li)
            if abc_bf is None:
                return ("st", ev)
            nc.scalar.wait_ge(st.sem, ev)
            sc.inc(nc.scalar.copy(abc_bf[:, :], bankA[:, 0:512]))
            li = nc.scalar.copy(bbc_bf[:, :], bankB[:, 0:512])
            ev2 = sc.inc(li)
            bank_r(bankA, sc, ev2)
            bank_r(bankB, sc, ev2)
            return ("sc", ev2)

        def ln_apply(x_bf_t, abc_bf, bbc_bf, dst, ev_abc):
            eng_tag, ev = ev_abc
            nc.vector.wait_ge(sc.sem if eng_tag == "sc" else st.sem, ev)
            for cc in range(CCH):
                nc.vector.tensor_tensor(dst[:, cc * 512:(cc + 1) * 512],
                                        x_bf_t[:, cc * 512:(cc + 1) * 512],
                                        abc_bf[:, :], ALU.mult)
            vguard()
            for cc in range(CCH):
                li = nc.vector.tensor_tensor(dst[:, cc * 512:(cc + 1) * 512],
                                             dst[:, cc * 512:(cc + 1) * 512],
                                             bbc_bf[:, :], ALU.subtract)
                ev = sv.inc(li)
            return ev

        def phase_barrier():
            marks = [(st, st.n), (sv, sv.n), (sc, sc.n), (sg, sg.n), (sy, sy.n)]
            for eng in (nc.tensor, nc.vector, nc.scalar, nc.gpsimd, nc.sync):
                for ctr, val in marks:
                    eng.wait_ge(ctr.sem, val)

        sy.dma(nc.sync, cbf[:, :], cbfH[:, :])
        sy.dma(nc.sync, cf32[:, :], cf32H[:, :])
        sy.dma(nc.sync, bproj[:, :], bprojH[:, :])
        ev_consts_sy = sy.n
        sg.dma(nc.gpsimd, rw[:, :], rwH[:, :])
        ev_rw = sg.n
        nc.vector.memset(epsb[:, :], EPS_ATTN)
        ev_epsb = sv.inc(nc.vector.engine_nop())

        w1 = Ecm(nc.sbuf_tensor("w1", [128, CCH * 2048], BF16))
        with ExitStack() as actx:
            Ea = actx.enter_context
            wqkv = Ea(nc.sbuf_tensor("wqkv", [128, CCH * 3072], BF16))
            x_bf2 = [Ea(nc.sbuf_tensor(f"xbfA{i}", [128, 4096], BF16)) for i in range(2)]
            xsq2 = [Ea(nc.sbuf_tensor(f"xsqA{i}", [128, 4096], BF16)) for i in range(2)]
            xln2 = [Ea(nc.sbuf_tensor(f"xlnA{i}", [128, 4096], BF16)) for i in range(2)]
            k_bf2 = [Ea(nc.sbuf_tensor(f"kbf{i}", [128, 4096], BF16)) for i in range(2)]
            v33_2 = [Ea(nc.sbuf_tensor(f"v33_{i}", [128, 4224], BF16)) for i in range(2)]
            a_bcA = Ea(nc.sbuf_tensor("a_bcA", [128, 512], BF16))
            b_bcA = Ea(nc.sbuf_tensor("b_bcA", [128, 512], BF16))
            vk_sb = Ea(nc.sbuf_tensor("vk_sb", [128, 8 * 132], F32))

            ev_xbf, ev_xln_t, ev_stats_t, ev_abc_t = {}, {}, {}, {}
            ev_kvevac, ev_vk_sub = {}, {}

            def load_x(t):
                if t >= 2:
                    nc.gpsimd.wait_ge(sv.sem, ev_xln_t[t - 2])
                sg.dma(nc.gpsimd,
                    AP(x_bf2[t % 2], 0, [[4096, 128], [512, CCH], [1, 512]]),
                    AP(xT, t * 512, [[N, 128], [128 * N, CCH], [1, 512]]))
                ev_xbf[t] = sg.n

            def ln_chain_a(t):
                bi = t % 2
                nc.vector.wait_ge(sg.sem, ev_xbf[t])
                if t >= 2:
                    nc.vector.wait_ge(st.sem, ev_stats_t[t - 1])
                li = nc.vector.tensor_tensor(xsq2[bi][:, :], x_bf2[bi][:, :],
                                             x_bf2[bi][:, :], ALU.mult)
                ev_sq = sv.inc(li)
                if t == 0:
                    nc.tensor.wait_ge(sy.sem, ev_consts_sy)
                ev_ab = ln_stats(x_bf2[bi], xsq2[bi], ev_sq, ps[3], ps[4],
                                 ev_sc_free=ev_abc_t.get(t - 1))
                ev_stats_t[t] = st.n
                return ev_ab

            def ln_finish_a(t, ev_ab):
                bi = t % 2
                ev_abc = ln_bcast(ev_ab, ps[3], ps[4], a_bcA, b_bcA)
                ev_abc_t[t] = ev_abc[1]
                if t >= 2:
                    nc.vector.wait_ge(st.sem, ev_kv_last_mm[t - 2])
                ev_xln_t[t] = ln_apply(x_bf2[bi], a_bcA, b_bcA, xln2[bi], ev_abc)

            def kv_sub(t, sub):
                bi = t % 2
                for og in range(4):
                    gi = t * 16 + sub * 4 + og
                    if t == 0 and sub == 0:
                        nc.tensor.wait_ge(sg.sem, wqkv_parts[og])
                    bk = ps[5 + (gi % 3)]
                    if gi - 3 in ev_kvevac:
                        nc.tensor.wait_ge(sc.sem, ev_kvevac[gi - 3])
                    for cc in range(CCH):
                        li = nc.tensor.matmul(
                            bk[:, 0:512],
                            xln2[bi][:, cc * 512 + sub * 128:
                                     cc * 512 + sub * 128 + 128],
                            wqkv[:, cc * 3072 + 1024 + 512 * og:
                                 cc * 3072 + 1024 + 512 * (og + 1)],
                            start=(cc == 0), stop=(cc == CCH - 1))
                    ev_mm = st.inc(li)
                    ev_kv_last_mm[t] = ev_mm
                    nc.scalar.wait_ge(st.sem, ev_mm)
                    if sub == 0 and og == 0:
                        if t <= 1:
                            nc.scalar.wait_ge(sv.sem, ev_v33init)
                        if (t - 2, 3) in ev_vk_sub:
                            nc.scalar.wait_ge(st.sem, ev_vk_sub[(t - 2, 3)])
                    if og < 2:
                        li = nc.scalar.activation(
                            k_bf2[bi][:, sub * 1024 + og * 512:
                                      sub * 1024 + (og + 1) * 512],
                            bk[:, 0:512], AF.Relu)
                    else:
                        li = nc.scalar.copy(
                            AP(v33_2[bi], sub * 1056 + (og - 2) * 528,
                               [[4224, 128], [33, 16], [1, 32]]),
                            bk[:, 0:512])
                    ev_kvevac[gi] = sc.inc(li)

                nc.tensor.wait_ge(sc.sem, ev_kvevac[t * 16 + sub * 4 + 3])
                for gg in range(8):
                    li = nc.tensor.matmul(
                        ps[gg // 3][:, (gg % 3) * 132:(gg % 3) * 132 + 132],
                        k_bf2[bi][:, sub * 1024 + gg * 128:
                                  sub * 1024 + (gg + 1) * 128],
                        AP(v33_2[bi], sub * 1056 + gg * 132,
                           [[4224, 128], [1, 132]]),
                        start=(t == 0 and sub == 0 and gg % 3 == 0),
                        stop=(t == NT_A - 1 and sub == 3
                              and (gg % 3 == 2 or gg == 7)))
                ev_vk_sub[(t, sub)] = st.inc(li)

            ev_kv_last_mm = {}
            wqkv_parts = [None] * 4

            def load_wqkv_part(i):
                sg.dma(nc.gpsimd,
                    AP(wqkv, 1024 + i * 512,
                       [[CCH * 3072, 128], [3072, CCH], [1, 512]]),
                    AP(wqkvT, 1024 + i * 512,
                       [[3072, 128], [128 * 3072, CCH], [1, 512]]))
                wqkv_parts[i] = sg.n

            load_x(0)
            load_wqkv_part(0)
            load_x(1)
            for i in range(1, 4):
                load_wqkv_part(i)
            ev_wqkv = sg.n
            for i in range(2):
                li = nc.vector.memset(
                    AP(v33_2[i], 32, [[4224, 128], [33, 128], [1, 1]]), 1.0)
                ev_v33init = sv.inc(li)

            ev_ab0 = ln_chain_a(0)
            ln_finish_a(0, ev_ab0)

            for t in range(NT_A):
                nc.tensor.wait_ge(sv.sem, ev_xln_t[t])
                if t + 2 < NT_A:
                    load_x(t + 2)
                kv_sub(t, 0)
                ev_ab_n = ln_chain_a(t + 1) if t + 1 < NT_A else None
                kv_sub(t, 1)
                kv_sub(t, 2)
                if ev_ab_n is not None:
                    ln_finish_a(t + 1, ev_ab_n)
                kv_sub(t, 3)
                if t == NT_A - 3:
                    for i in range(2):
                        sg.dma(nc.gpsimd,
                            AP(w1, i * 4 * 1024,
                               [[CCH * 2048, 128], [1024, 4], [1, 1024]]),
                            AP(wqkvT, i * 4 * 128 * 3072,
                               [[3072, 128], [128 * 3072, 4], [1, 1024]]))
                    for i in range(2):
                        sg.dma(nc.gpsimd,
                            AP(w1, CCH * 1024 + i * 4 * 1024,
                               [[CCH * 2048, 128], [1024, 4], [1, 1024]]),
                            AP(wprojT, i * 4 * 128 * 1024,
                               [[1024, 128], [128 * 1024, 4], [1, 1024]]))
                    ev_wB_g = sg.n

            ev_vk_done = ev_vk_sub[(NT_A - 1, 3)]
            ev_wB = ev_wB_g

            nc.scalar.wait_ge(st.sem, ev_vk_done)
            for gg in range(8):
                li = nc.scalar.copy(
                    vk_sb[:, gg * 132:(gg + 1) * 132],
                    ps[gg // 3][:, (gg % 3) * 132:(gg % 3) * 132 + 132])
                ev_vkevac = sc.inc(li)

            nc.vector.memset(vkdiag[:, :], 0.0)
            nc.vector.memset(ks_st[:, :], 0.0)
            vguard()
            nc.vector.wait_ge(sc.sem, ev_vkevac)
            for gg in range(8):
                for hl in range(4):
                    r0 = 32 * hl
                    sv.inc(nc.vector.tensor_copy(
                        vkdiag[r0:r0 + 32, gg * 128 + r0:gg * 128 + r0 + 32],
                        vk_sb[r0:r0 + 32, gg * 132 + 33 * hl:
                              gg * 132 + 33 * hl + 32]))
                    li = nc.vector.tensor_copy(
                        ks_st[r0:r0 + 32, gg * 128 + 4 * gg + hl:
                              gg * 128 + 4 * gg + hl + 1],
                        vk_sb[r0:r0 + 32, gg * 132 + 33 * hl + 32:
                              gg * 132 + 33 * hl + 33])
                    ev_stationaries = sv.inc(li)

            dump("vk_sb", vk_sb[:, :], (128, 8 * 132))
            dump("vkdiag", vkdiag[:, :], (128, 8 * 128), BF16)
            dump("ks_st", ks_st[:, :], (128, 8 * 128), BF16)
            dump("xlnA", xln2[1][:, :], (128, 4096), BF16)
            dump("k_bf", k_bf2[1][:, :], (128, 4096), BF16)
            dump("v33", v33_2[1][:, :], (128, 4224), BF16)

            if stop_after == "A":
                nc.sync.wait_ge(sv.sem, sv.n)
                sy.dma(nc.sync, outT[0:128, 0:128], cf32[:, :])
                nc.sync.wait_ge(sy.sem, sy.n)
                return nc, dbg_specs

        acc = Ecm(nc.sbuf_tensor("acc", [128, NT * 4096], F32))
        y2 = [Ecm(nc.sbuf_tensor(f"y2_{i}", [128, 4096], BF16)) for i in range(2)]
        up_bf = w1
        gt_bf = [Ecm(nc.sbuf_tensor(f"gt{i}", [128, 512], BF16)) for i in range(2)]
        with ExitStack() as bctx:
            Eb = bctx.enter_context
            x_bfB = Eb(nc.sbuf_tensor("x_bfB", [128, 4096], BF16))
            nc.gpsimd.wait_ge(st.sem, st.n)
            sg.dma(nc.gpsimd,
                AP(x_bfB, 0, [[4096, 128], [512, CCH], [1, 512]]),
                AP(xT, 0, [[N, 128], [128 * N, CCH], [1, 512]]))
            ev_xbfB0 = sg.n
            phase_barrier()
            xf32h = Eb(nc.sbuf_tensor("xf32h", [128, 1024], F32))
            xsqB = Eb(nc.sbuf_tensor("xsqB", [128, 4096], BF16))
            xlnB = Eb(nc.sbuf_tensor("xlnB", [128, 4096], BF16))
            q2 = [Eb(nc.sbuf_tensor(f"q2_{i}", [128, 4096], BF16)) for i in range(2)]
            x2bfB = Eb(nc.sbuf_tensor("x2bfB", [128, 4096], BF16))
            rn_bf = Eb(nc.sbuf_tensor("rn_bf", [32, 512], BF16))
            rn_sb = Eb(nc.sbuf_tensor("rn_sb", [32, 512], F32))
            qp_bf = [Eb(nc.sbuf_tensor(f"qp{i}", [128, 512], BF16)) for i in range(2)]
            lg_sb = Eb(nc.sbuf_tensor("lg_sb", [8, 512], F32))
            gl_sb = Eb(nc.sbuf_tensor("gl_sb", [128, 32], F32))
            gcmp = Eb(nc.sbuf_tensor("gcmp", [128, 32], F32))
            grank = Eb(nc.sbuf_tensor("grank", [128, 32], F32))
            gmask = Eb(nc.sbuf_tensor("gmask", [128, 32], F32))
            gex = Eb(nc.sbuf_tensor("gex", [128, 32], F32))
            gred = Eb(nc.sbuf_tensor("gred", [128, 4], F32))
            gtile2 = [Eb(nc.sbuf_tensor(f"gtile{i}", [128, 32], BF16)) for i in range(2)]
            attn_bf = Eb(nc.sbuf_tensor("attn_bf", [128, 4096], BF16))
            gh2 = [_PairBuf(attn_bf, None, 4096), None]

            EB = {}
            EB[("xbf", 0)] = ev_xbfB0

            def s1_stats(t):
                nc.vector.wait_ge(sg.sem, EB[("xbf", t)])
                nc.vector.wait_ge(st.sem, EB.get(("stats2mm", t - 3), 0))
                li = nc.vector.tensor_tensor(xsqB[:, :], x_bfB[:, :],
                                             x_bfB[:, :], ALU.mult)
                EB[("sq1", t)] = sv.inc(li)
                if seqb is True or 'post_sq1' in seqb:
                    phase_barrier()
                EB[("ab1", t)] = ln_stats(x_bfB, xsqB, EB[("sq1", t)],
                                          ps[3], ps[4], half=0)
                EB[("stats1mm", t)] = st.n

            def s2a_attn(t):
                nc.sync.wait_ge(sv.sem, EB.get(("resid", t - 1), 0))
                sy.dma(nc.sync,
                    AP(xf32h, 0, [[1024, 128], [512, 2], [1, 512]]),
                    AP(xT, t * 512, [[N, 128], [128 * N, 2], [1, 512]]))
                EB[("xf0", t)] = sy.n
                tbar("norm")
                nc.tensor.wait_ge(sc.sem, EB[("qdone", t)])
                if t == 0:
                    nc.tensor.wait_ge(sv.sem, ev_stationaries)
                bank_w(ps[0])
                for cc in range(CCH):
                    li = nc.tensor.matmul(ps[0][0:128, 0:512],
                                          ks_st[:, cc * 128:(cc + 1) * 128],
                                          q2[t % 2][:, cc * 512:(cc + 1) * 512],
                                          start=(cc == 0), stop=(cc == CCH - 1))
                ev_n = st.inc(li)
                nc.vector.wait_ge(st.sem, ev_n)
                nc.vector.tensor_scalar(rn_sb[:, :], ps[0][0:32, 0:512],
                                        EPS_ATTN, None, ALU.add)
                vguard()
                nc.vector.reciprocal(rn_sb[:, :], rn_sb[:, :])
                vguard()
                li = nc.vector.tensor_copy(rn_bf[:, :], rn_sb[:, :])
                ev_rn = sv.inc(li)
                bank_r(ps[0], sv, ev_rn)
                tbar("attn")
                nc.tensor.wait_ge(sv.sem, ev_rn)
                for cc in range(CCH):
                    bkR = ps[1 + cc % 2]
                    bkM = ps[3 + cc % 2]
                    bank_w(bkR)
                    li = nc.tensor.matmul(
                        bkR[:, 0:512],
                        cbf[0:32, CB_SELH + cc * 128:CB_SELH + (cc + 1) * 128],
                        rn_bf[:, :], start=True, stop=True)
                    evR = st.inc(li)
                    nc.vector.wait_ge(st.sem, evR)
                    li = nc.vector.tensor_tensor(qp_bf[cc % 2][:, :],
                                                 q2[t % 2][:, cc * 512:(cc + 1) * 512],
                                                 bkR[:, 0:512], ALU.mult)
                    ev_qp = sv.inc(li)
                    bank_r(bkR, sv, ev_qp)
                    nc.tensor.wait_ge(sv.sem, ev_qp)
                    bank_w(bkM)
                    li = nc.tensor.matmul(bkM[:, 0:512],
                                          vkdiag[:, cc * 128:(cc + 1) * 128],
                                          qp_bf[cc % 2][:, :], start=True,
                                          stop=True)
                    evM = st.inc(li)
                    nc.scalar.wait_ge(st.sem, evM)
                    li = nc.scalar.copy(attn_bf[:, cc * 512:(cc + 1) * 512],
                                        bkM[:, 0:512])
                    EB[("attnevac", t)] = sc.inc(li)
                    bank_r(bkM, sc, EB[("attnevac", t)])

            def s2b_prep(t):
                nc.scalar.wait_ge(sv.sem, EB[("resid", t)])
                nc.scalar.wait_ge(st.sem, EB.get(("stats2mm", t - 1), 0))
                nc.scalar.wait_ge(sv.sem, EB.get(("y", t - 1), 0))
                li = nc.scalar.copy(x2bfB[:, :], acc[:, t * 4096:(t + 1) * 4096])
                EB[("x2bf", t)] = sc.inc(li)
                nc.vector.wait_ge(sc.sem, EB[("x2bf", t)])
                nc.vector.wait_ge(st.sem, EB[("stats1mm", t + 2)] if
                                  (("stats1mm", t + 2) in EB) else 0)
                li = nc.vector.tensor_tensor(xsqB[:, :], x2bfB[:, :],
                                             x2bfB[:, :], ALU.mult)
                EB[("sq2", t)] = sv.inc(li)
            def s2a_proj(t):
                tbar("proj")
                nc.tensor.wait_ge(sc.sem, EB[("attnevac", t)])
                for oc in range(CCH):
                    bk = ps[5 + oc % 2]
                    bank_w(bk)
                    for cc in range(CCH):
                        li = nc.tensor.matmul(
                            bk[:, 0:512],
                            w1[:, CCH * 1024 + cc * 1024 + oc * 128:
                               CCH * 1024 + cc * 1024 + (oc + 1) * 128],
                            attn_bf[:, cc * 512:(cc + 1) * 512],
                            start=(cc == 0), stop=(cc == CCH - 1))
                    evt = st.inc(li)
                    nc.scalar.wait_ge(st.sem, evt)
                    li = nc.scalar.activation(
                        acc[:, t * 4096 + oc * 512:t * 4096 + (oc + 1) * 512],
                        bk[:, 0:512], AF.Identity, bias=bproj[:, oc:oc + 1])
                    EB[("pj", t, oc)] = sc.inc(li)
                    bank_r(bk, sc, EB[("pj", t, oc)])
                EB[("projmm", t)] = st.n
                ev_q = EB[("xf0", t)]
                for qq in range(4):
                    nc.vector.wait_ge(sc.sem, EB[("pj", t, 2 * qq + 1)])
                    nc.vector.wait_ge(sy.sem, ev_q)
                    for oc in (2 * qq, 2 * qq + 1):
                        li = nc.vector.tensor_tensor(
                            acc[:, t * 4096 + oc * 512:t * 4096 + (oc + 1) * 512],
                            acc[:, t * 4096 + oc * 512:t * 4096 + (oc + 1) * 512],
                            xf32h[:, (oc % 2) * 512:(oc % 2 + 1) * 512], ALU.add)
                        ev_r = sv.inc(li)
                    if qq < 3:
                        nc.sync.wait_ge(sv.sem, ev_r)
                        sy.dma(nc.sync,
                            AP(xf32h, 0, [[1024, 128], [512, 2], [1, 512]]),
                            AP(xT, t * 512 + (2 * qq + 2) * 128 * N,
                               [[N, 128], [128 * N, 2], [1, 512]]))
                        ev_q = sy.n
                        sy_n_chk = sy.n
                EB[("resid", t)] = ev_r

            def s1_finish(t):
                tbar("bcast1")
                EB[("abc1", t)] = ln_bcast(EB[("ab1", t)], ps[3], ps[4],
                                           None, None, half=0)
                nc.vector.wait_ge(st.sem, EB.get(("qmm", t - 1), 0))
                EB[("xln1", t)] = ln_apply(x_bfB, ps[3][:, 0:512],
                                           ps[4][:, 0:512], xlnB,
                                           EB[("abc1", t)])
                if t + 1 < NT:
                    nc.gpsimd.wait_ge(sv.sem, EB[("xln1", t)])
                    sg.dma(nc.gpsimd,
                        AP(x_bfB, 0, [[4096, 128], [512, CCH], [1, 512]]),
                        AP(xT, (t + 1) * 512, [[N, 128], [128 * N, CCH], [1, 512]]))
                    EB[("xbf", t + 1)] = sg.n
                bank_r(ps[3], sv, EB[("xln1", t)])
                bank_r(ps[4], sv, EB[("xln1", t)])

            def s1_qproj(t):
                tbar("q")
                nc.tensor.wait_ge(sv.sem, EB[("xln1", t)])
                if t == 0:
                    nc.tensor.wait_ge(sg.sem, ev_wB)
                for oc in range(CCH):
                    bk = ps[5 + oc % 2]
                    bank_w(bk)
                    for cc in range(CCH):
                        li = nc.tensor.matmul(
                            bk[:, 0:512],
                            w1[:, cc * 1024 + oc * 128:cc * 1024 + (oc + 1) * 128],
                            xlnB[:, cc * 512:(cc + 1) * 512],
                            start=(cc == 0), stop=(cc == CCH - 1))
                    evt = st.inc(li)
                    nc.scalar.wait_ge(st.sem, evt)
                    li = nc.scalar.activation(
                        q2[t % 2][:, oc * 512:(oc + 1) * 512],
                        bk[:, 0:512], AF.Relu)
                    ev_qe = sc.inc(li)
                    bank_r(bk, sc, ev_qe)
                EB[("qdone", t)] = ev_qe
                EB[("qmm", t)] = st.n

            def s2b_stats(t):
                tbar("stats2")
                EB[("ab2", t)] = ln_stats(x2bfB, xsqB, EB[("sq2", t)],
                                          ps[7], ps[5], half=0)
                EB[("stats2mm", t)] = st.n
            def s2b_finish(t):
                EB[("abc2", t)] = ln_bcast(EB[("ab2", t)], ps[1], ps[2],
                                           None, None, half=0)
                nc.vector.wait_ge(sy.sem, EB.get(("yst", t - 2), 0))
                nc.vector.wait_ge(st.sem, EB.get(("routermm", t - 2), 0))
                EB[("y", t)] = ln_apply(x2bfB, ps[1][:, 0:512],
                                        ps[2][:, 0:512], y2[t % 2],
                                        EB[("abc2", t)])
                bank_r(ps[1], sv, EB[("y", t)])
                bank_r(ps[2], sv, EB[("y", t)])
                nc.sync.wait_ge(sv.sem, EB[("y", t)])
                sy.dma(nc.sync,
                    AP(yscr, t * 512, [[TOK, 128], [128 * TOK, CCH], [1, 512]]),
                    AP(y2[t % 2], 0, [[4096, 128], [512, CCH], [1, 512]]))
                EB[("yst", t)] = sy.n
            def s2c_router(t):
                tbar("router")
                nc.tensor.wait_ge(sv.sem, EB[("y", t)])
                if t == 0:
                    nc.tensor.wait_ge(sg.sem, ev_rw)
                bank_w(ps[7])
                for cc in range(CCH):
                    li = nc.tensor.matmul(ps[7][0:8, 0:512],
                                          rw[:, cc * 8:(cc + 1) * 8],
                                          y2[t % 2][:, cc * 512:(cc + 1) * 512],
                                          start=(cc == 0), stop=(cc == CCH - 1))
                ev_lg = st.inc(li)
                EB[("routermm", t)] = ev_lg
                nc.scalar.wait_ge(st.sem, ev_lg)
                li = nc.scalar.copy(lg_sb[:, :], ps[7][0:8, 0:512])
                ev_lge = sc.inc(li)
                bank_r(ps[7], sc, ev_lge)
                nc.tensor.wait_ge(sc.sem, ev_lge)
                bank_w(ps[0])
                for sub in range(4):
                    li = nc.tensor.matmul(ps[0][0:128, sub * 8:(sub + 1) * 8],
                                          lg_sb[0:8, sub * 128:(sub + 1) * 128],
                                          cf32[0:8, 0:8], start=True, stop=True)
                ev_tp = st.inc(li)
                nc.vector.wait_ge(st.sem, ev_tp)
                li = nc.vector.tensor_copy(gl_sb[:, :], ps[0][0:128, 0:32])
                ev_gl = sv.inc(li)
                bank_r(ps[0], sv, ev_gl)
                nc.vector.wait_ge(sv.sem, ev_gl)
                nc.vector.memset(grank[:, :], 0.0)
                vguard()
                for j in range(E):
                    for sub in range(4):
                        nc.vector.tensor_scalar(
                            gcmp[:, sub * 8:(sub + 1) * 8],
                            gl_sb[:, sub * 8:(sub + 1) * 8],
                            gl_sb[:, sub * 8 + j:sub * 8 + j + 1], None,
                            ALU.is_lt)
                    vguard()
                    for sub in range(4):
                        nc.vector.tensor_tensor(
                            grank[:, sub * 8:(sub + 1) * 8],
                            grank[:, sub * 8:(sub + 1) * 8],
                            gcmp[:, sub * 8:(sub + 1) * 8], ALU.add)
                    vguard()
                nc.vector.tensor_scalar(gmask[:, :], grank[:, :], 3.5, None,
                                        ALU.is_lt)
                for sub in range(4):
                    li = nc.vector.reduce_max(gred[:, sub:sub + 1],
                                              gl_sb[:, sub * 8:(sub + 1) * 8],
                                              axis=AXX, negate=True)
                    ev_gm = sv.inc(li)
                nc.scalar.wait_ge(sv.sem, ev_gm)
                for sub in range(4):
                    li = nc.scalar.activation(gex[:, sub * 8:(sub + 1) * 8],
                                              gl_sb[:, sub * 8:(sub + 1) * 8],
                                              AF.Exp, bias=gred[:, sub:sub + 1])
                ev_gex = sc.inc(li)
                nc.vector.wait_ge(sc.sem, ev_gex)
                nc.vector.tensor_tensor(gex[:, :], gex[:, :], gmask[:, :],
                                        ALU.mult)
                vguard()
                for sub in range(4):
                    nc.vector.reduce_sum(gred[:, sub:sub + 1],
                                         gex[:, sub * 8:(sub + 1) * 8], axis=AXX)
                vguard()
                nc.vector.reciprocal(gred[:, :], gred[:, :])
                vguard()
                for sub in range(4):
                    li = nc.vector.tensor_scalar(
                        gtile2[t % 2][:, sub * 8:(sub + 1) * 8],
                        gex[:, sub * 8:(sub + 1) * 8],
                        gred[:, sub:sub + 1], None, ALU.mult)
                    EB[("gt", t)] = sv.inc(li)
            def s2c_gtp(t):
                nc.tensor.wait_ge(sv.sem, EB[("gt", t)])
                bank_w(ps[2])
                for sub in range(4):
                    li = nc.tensor.matmul(ps[2][0:8, sub * 128:(sub + 1) * 128],
                                          gtile2[t % 2][:, sub * 8:(sub + 1) * 8],
                                          ident_bf[:, 0:128], start=True,
                                          stop=True)
                ev_gtp = st.inc(li)
                nc.scalar.wait_ge(st.sem, ev_gtp)
                li = nc.scalar.copy(gates_fm[0:8, t * 512:(t + 1) * 512],
                                    ps[2][0:8, 0:512])
                EB[("gfm", t)] = sc.inc(li)
                bank_r(ps[2], sc, EB[("gfm", t)])

            ev_up_used, ev_dn_used, EB_C = {}, {}, {}
            ev_gelu, ev_gh, ev_acc = {}, {}, {}

            def c_gate(s, t):
                e = s // 2
                if s == 0:
                    nc.tensor.wait_ge(sc.sem, EB[("gfm", t)])
                bank_w(ps[4 + t % 2])
                li = nc.tensor.matmul(
                    ps[4 + t % 2][:, 0:512],
                    cbf[0:8, CB_SELG + e * 128:CB_SELG + (e + 1) * 128],
                    gates_fm[0:8, t * 512:(t + 1) * 512],
                    start=True, stop=True)
                EB_C[("gmm", t)] = st.inc(li)

            def c_up(s, t, first=False, hcs=None):
                hcs = range(HCH) if hcs is None else hcs
                ysrc = lambda cc: y2[t % 2][:, cc * 512:(cc + 1) * 512]
                if s == 0 and t >= 2 and hcs[0] == 0:
                    nc.tensor.wait_ge(sv.sem, EB[("y", t)])
                for hc in hcs:
                    if first and hc % 4 == 0:
                        nc.tensor.wait_ge(sg.sem, EB_C["up_parts"][hc // 4])
                    bk = ps[hc % 2]
                    bank_w(bk)
                    pv = (s, t, hc - 2) if hc >= 2 else (
                        (s, t - 1, HCH - 2 + hc) if t >= 1 else
                        (s - 1, NT - 1, HCH - 2 + hc))
                    if pv in ev_gelu:
                        nc.tensor.wait_ge(sc.sem, ev_gelu[pv])
                    for cc in range(CCH):
                        li = nc.tensor.matmul(
                            bk[:, 0:512],
                            up_bf[:, cc * 2048 + hc * 128:
                                  cc * 2048 + (hc + 1) * 128],
                            ysrc(cc),
                            start=(cc == 0), stop=(cc == CCH - 1))
                    evu = st.inc(li)
                    nc.scalar.wait_ge(st.sem, evu)
                    if pv in ev_gh:
                        nc.scalar.wait_ge(sv.sem, ev_gh[pv])
                    li = nc.scalar.activation(gt_bf[hc % 2][:, :], bk[:, 0:512],
                                              AF.Gelu_apprx_tanh)
                    ev_gelu[(s, t, hc)] = sc.inc(li)
                    nc.vector.wait_ge(sc.sem, ev_gelu[(s, t, hc)])
                    if hc == 0:
                        nc.vector.wait_ge(st.sem, EB_C[("gmm", t)])
                        nc.vector.wait_ge(st.sem, EB_C.get(("ghfree", t % 2), 0))
                    li = nc.vector.tensor_tensor(
                        gh2[t % 2][:, hc * 512:(hc + 1) * 512],
                        gt_bf[hc % 2][:, :], ps[4 + t % 2][:, 0:512], ALU.mult)
                    ev_gh[(s, t, hc)] = sv.inc(li)
                bank_r(ps[4 + t % 2], sv, ev_gh[(s, t, hcs[-1])])
                ev_up_used[s] = st.n

            def c_down(s, t):
                nc.tensor.wait_ge(sv.sem, ev_gh[(s, t, HCH - 1)])
                for oc in range(CCH):
                    if EB_C.pop("dn_first_pending", None) == (s,) and False:
                        pass
                    if EB_C.get("dn_first") == (s,) and oc % 2 == 0:
                        nc.tensor.wait_ge(sg.sem, EB_C["dn_parts"][oc // 2])
                    bk = ps[2 + oc % 2]
                    bank_w(bk)
                    pv = (s, t, oc - 2) if oc >= 2 else (
                        (s, t - 1, CCH - 2 + oc) if t >= 1 else
                        (s - 1, NT - 1, CCH - 2 + oc))
                    if pv in ev_acc:
                        nc.tensor.wait_ge(sv.sem, ev_acc[pv])
                    for hc in range(HCH):
                        li = nc.tensor.matmul(
                            bk[:, 0:512],
                            dn_bf[:, hc * 1024 + oc * 128:
                                  hc * 1024 + (oc + 1) * 128],
                            gh2[t % 2][:, hc * 512:(hc + 1) * 512],
                            start=(hc == 0), stop=(hc == HCH - 1))
                    evd = st.inc(li)
                    nc.vector.wait_ge(st.sem, evd)
                    li = nc.vector.tensor_tensor(
                        acc[:, t * 4096 + oc * 512:t * 4096 + (oc + 1) * 512],
                        acc[:, t * 4096 + oc * 512:t * 4096 + (oc + 1) * 512],
                        bk[:, 0:512], ALU.add)
                    ev_acc[(s, t, oc)] = sv.inc(li)
                    bank_r(bk, sv, ev_acc[(s, t, oc)])
                    if s == 2 * E - 1:
                        nc.gpsimd.wait_ge(sv.sem, ev_acc[(s, t, oc)])
                        sg.dma(nc.gpsimd,
                            AP(outT, t * 512 + oc * 128 * TOK,
                               [[TOK, 128], [1, 512]]),
                            AP(acc, t * 4096 + oc * 512,
                               [[NT * 4096, 128], [1, 512]]))
                EB_C[("ghfree", t % 2)] = st.n
                ev_dn_used[s] = st.n

            seqb = getattr(build_program, "SEQB", ())

            def maybe_bar(name):
                if seqb is True or name in seqb:
                    phase_barrier()
                mode = getattr(build_program, "BARMODE", None)
                if mode and name == "s1_stats":
                    marks = {"T": (st, st.n), "V": (sv, sv.n), "S": (sc, sc.n),
                             "G": (sg, sg.n), "Y": (sy, sy.n)}
                    engs = {"t": nc.tensor, "v": nc.vector, "s": nc.scalar,
                            "g": nc.gpsimd, "y": nc.sync}
                    for pair in mode.split(","):
                        e, cs = pair.split(":")
                        for ch in cs:
                            ctr, val = marks[ch]
                            engs[e].wait_ge(ctr.sem, val)

            LAGS = (0, 1, 2, 1, 0, 3, 2, 4, 2)
            lg_s1s, lg_attn, lg_prep, lg_proj, lg_s1q, lg_rt, lg_s2bs, lg_gtp, lg_fin = LAGS
            for i in range(NT + max(LAGS)):
                if 0 <= i - lg_s1s < NT:
                    maybe_bar('pre_s1'); s1_stats(i - lg_s1s); maybe_bar('s1_stats')
                if 0 <= i - lg_attn < NT:
                    s2a_attn(i - lg_attn); maybe_bar('s2a_attn')
                if 0 <= i - lg_s1q < NT:
                    s1_finish(i - lg_s1q)
                if 0 <= i - lg_prep < NT:
                    s2b_prep(i - lg_prep); maybe_bar('s2b_prep')
                if 0 <= i - lg_proj < NT:
                    s2a_proj(i - lg_proj); maybe_bar('s2a_proj')
                if 0 <= i - lg_s1q < NT:
                    s1_qproj(i - lg_s1q); maybe_bar('s1_apply_qproj')
                if 0 <= i - lg_gtp < NT - 1:
                    s2c_gtp(i - lg_gtp); maybe_bar('s2c_gtp')
                if i == 6:
                    c_gate(0, 2)
                    c_up(0, 2, first=True, hcs=range(8))
                if 0 <= i - lg_rt < NT:
                    s2c_router(i - lg_rt); maybe_bar('s2c_router')
                if 0 <= i - lg_s2bs < NT:
                    s2b_stats(i - lg_s2bs); maybe_bar('s2b_stats')
                if 0 <= i - lg_fin < NT:
                    s2b_finish(i - lg_fin); maybe_bar('s2b_finish')
                if i == 3:
                    nc.gpsimd.wait_ge(st.sem, EB[("qmm", NT - 1)])
                    for j in range(4):
                        sg.dma(nc.gpsimd,
                            AP(up_bf, j * 512,
                               [[CCH * 2048, 128], [2048, 4], [1, 512]]),
                            AP(upw, j * 512,
                               [[Hf, 128], [128 * Hf, 4], [1, 512]]))
                if i == 4:
                    nc.gpsimd.wait_ge(st.sem, EB[("projmm", NT - 1)])
                    up_parts0 = []
                    for j in range(4):
                        sg.dma(nc.gpsimd,
                            AP(up_bf, 4 * 2048 + j * 512,
                               [[CCH * 2048, 128], [2048, 4], [1, 512]]),
                            AP(upw, 4 * 128 * Hf + j * 512,
                               [[Hf, 128], [128 * Hf, 4], [1, 512]]))
                        up_parts0.append(sg.n)
                    EB_C["up_parts"] = up_parts0

            ev_y_all_st = EB[("yst", NT - 1)]
            ev_projmm_last = EB[("projmm", NT - 1)]
            ev_stats2mm_last = EB[("stats2mm", NT - 1)]
            ev_y_last = EB[("y", NT - 1)]
            ev_router_last = EB[("routermm", NT - 1)]
            dump("gates_fm", gates_fm[:, :], (8, TOK), BF16)
            dump("accB", acc[:, :], (128, NT * 4096))

            if stop_after == "B":
                nc.sync.wait_ge(sc.sem, sc.n)
                nc.sync.wait_ge(sv.sem, sv.n)
                sy.dma(nc.sync, outT[0:128, 0:128], cf32[:, :])
                nc.sync.wait_ge(sy.sem, sy.n)
                return nc, dbg_specs

        ev_b_end = {"st": st.n, "sv": sv.n, "sc": sc.n, "sy": sy.n, "sg": sg.n}
        with ExitStack() as cctx:
            Ec = cctx.enter_context
            dn_bf = Ec(nc.sbuf_tensor("dn_bf", [128, HCH * 1024], BF16))
            gh2[1] = Ec(nc.sbuf_tensor("gh1", [128, HCH * 512], BF16))
            gh2[0].hi = Ec(nc.sbuf_tensor("gh0hi", [128, 4096], BF16))

            for s in range(2 * E):
                e, hh = s // 2, s % 2
                if s >= 1:
                    nc.gpsimd.wait_ge(st.sem, ev_up_used[s - 1])
                    up_parts = []
                    for i in range(4):
                        sg.dma(nc.gpsimd,
                            AP(up_bf, i * 512,
                               [[CCH * 2048, 128], [2048, CCH], [1, 512]]),
                            AP(upw, e * C * Hf + hh * 2048 + i * 512,
                               [[Hf, 128], [128 * Hf, CCH], [1, 512]]))
                        up_parts.append(sg.n)
                else:
                    up_parts = EB_C["up_parts"]
                if s >= 1:
                    nc.gpsimd.wait_ge(st.sem, ev_dn_used[s - 1])
                else:
                    nc.gpsimd.wait_ge(st.sem, ev_stats2mm_last)
                dn_parts = []
                for i in range(4):
                    sg.dma(nc.gpsimd,
                        AP(dn_bf, i * 256,
                           [[HCH * 1024, 128], [1024, HCH], [1, 256]]),
                        AP(downw, e * Hf * C + hh * 2048 * C + i * 256,
                           [[C, 128], [128 * C, HCH], [1, 256]]))
                    dn_parts.append(sg.n)
                EB_C["up_parts"], EB_C["dn_parts"] = up_parts, dn_parts
                if s == 0:
                    EB_C["resident"] = 1
                res = EB_C["resident"]
                firstseg = True
                for k, tp2 in enumerate([res, 1 - res]):
                    t0, t1 = 2 * tp2, 2 * tp2 + 1
                    if k == 1:
                        if EB_C.get("first_ys", True):
                            nc.sync.wait_ge(sy.sem, ev_y_all_st)
                            EB_C["first_ys"] = False
                        nc.sync.wait_ge(st.sem, EB_C["yfree"])
                        for j in (t0, t1):
                            sy.dma(nc.sync,
                                AP(y2[j % 2], 0, [[4096, 128], [512, CCH], [1, 512]]),
                                AP(yscr, j * 512, [[TOK, 128], [128 * TOK, CCH], [1, 512]]))
                        nc.tensor.wait_ge(sy.sem, sy.n)
                        EB_C["resident"] = 1 - res
                    if firstseg:
                        EB_C["dn_first"] = (s,)
                    if s == 0 and t0 == 2:
                        c_up(0, 2, first=True, hcs=range(8, HCH))
                        s2c_gtp(NT - 1)
                        c_down(s, t0)
                        EB_C.pop("dn_first", None)
                        c_gate(s, t1)
                        c_up(s, t1)
                        EB_C["yfree"] = st.n
                        c_down(s, t1)
                        firstseg = False
                        continue
                    c_gate(s, t0)
                    c_up(s, t0, first=firstseg)
                    c_gate(s, t1)
                    c_up(s, t1)
                    if k == 0:
                        EB_C["yfree"] = st.n
                    c_down(s, t0)
                    EB_C.pop("dn_first", None)
                    c_down(s, t1)
                    firstseg = False


        nc.sync.wait_ge(sg.sem, sg.n)
        nc.sync.wait_ge(sy.sem, sy.n)

    return nc, dbg_specs



def _prep_inputs(L0_attn_x, L0_attn_W_qkv, L0_attn_W_proj, L0_attn_b_proj,
                 L0_ffn_router_w, L0_ffn_up_w, L0_ffn_down_w):
    x = np.asarray(L0_attn_x, dtype=np.float32)
    wqkvT = np.ascontiguousarray(np.asarray(L0_attn_W_qkv, np.float32).T)
    wprojT = np.ascontiguousarray(np.asarray(L0_attn_W_proj, np.float32).T)
    bprojH = np.ascontiguousarray(
        np.asarray(L0_attn_b_proj, np.float32).reshape(8, 128).T)
    rwf = np.asarray(L0_ffn_router_w, np.float32)
    rwH = np.ascontiguousarray(
        rwf.reshape(8, 128, 8).transpose(1, 0, 2).reshape(128, 64))
    upwf = np.ascontiguousarray(np.asarray(L0_ffn_up_w, np.float32))
    downwf = np.ascontiguousarray(np.asarray(L0_ffn_down_w, np.float32))
    cbf, cf32 = host_consts()
    in_maps = []
    for core in range(8):
        b, half = core // 2, core % 2
        xTb = np.ascontiguousarray(x[b].T)
        if half:
            xTb = np.ascontiguousarray(np.roll(xTb, -TOK, axis=1))
        in_maps.append({
            "xT": xTb, "wqkvT": wqkvT, "wprojT": wprojT, "bprojH": bprojH,
            "rwH": rwH, "upw": upwf, "downw": downwf, "cbfH": cbf,
            "cf32H": cf32,
        })
    return in_maps


def kernel(L0_attn_x, L0_attn_W_qkv, L0_attn_W_proj, L0_attn_b_proj,
           L0_ffn_router_w, L0_ffn_up_w, L0_ffn_down_w):
    in_maps = _prep_inputs(L0_attn_x, L0_attn_W_qkv, L0_attn_W_proj,
                           L0_attn_b_proj, L0_ffn_router_w, L0_ffn_up_w,
                           L0_ffn_down_w)
    nc, _ = build_program()
    res = run_bass_kernel_spmd(nc, in_maps, core_ids=list(range(8)))
    out = np.empty((B, N, C), dtype=np.float32)
    for core in range(8):
        b, half = core // 2, core % 2
        out[b, half * TOK:(half + 1) * TOK, :] = res.results[core]["outT"].T
    return out
